# version 04
# baseline (speedup 1.0000x reference)
import numpy as np
from contextlib import ExitStack

import concourse.bass as bass
import concourse.mybir as mybir
from concourse.bass_utils import run_bass_kernel_spmd

F32 = mybir.dt.float32
BF16 = mybir.dt.bfloat16
AF = mybir.ActivationFunctionType
ALU = mybir.AluOpType
AX = mybir.AxisListType

T, D, DI, DS = 8192, 1024, 2048, 16
L = 512
NTT = L // 128
EPS = 1e-5
NEXP = 9


class Bld:

    ENGS = ("pe", "dv", "ac", "sy", "gp")
    SEM_OF = dict(pe="PE", dv="DV", ac="AC", gp="GPC")

    def __init__(self, nc):
        self.nc = nc
        self.q = {e: [] for e in self.ENGS}
        self.cnt = dict(PE=0, DV=0, AC=0, GPC=0, SY=0, GP=0)
        self.sem = {}

    def mksems(self, ctx):
        for name in self.cnt:
            self.sem[name] = ctx.enter_context(self.nc.semaphore(name))

    def wait(self, eng, sname, val):
        if val <= 0:
            return
        self.q[eng].append(("w", sname, int(val)))

    def op(self, eng, fn):
        sname = self.SEM_OF[eng]
        self.cnt[sname] += 1
        self.q[eng].append(("i", fn, sname, 1))
        return self.cnt[sname]

    def dma(self, eng, out, in_, **kw):
        sname = {"sy": "SY", "gp": "GP"}[eng]
        self.cnt[sname] += 16
        self.q[eng].append(("d", out, in_, sname, kw))
        return self.cnt[sname]

    def replay(self, key, eng):
        for item in self.q[key]:
            k = item[0]
            if k == "w":
                eng.wait_ge(self.sem[item[1]], item[2])
            elif k == "i":
                ins = item[1](eng)
                ins.then_inc(self.sem[item[2]], item[3])
            else:
                _, out, in_, sname, kw = item
                eng.dma_start(out=out, in_=in_, **kw).then_inc(self.sem[sname], 16)


def build(n_chunks=16, tap=None, ffn_ems=None, tap_chunk=0):
    nc = bass.Bass("TRN2", target_bir_lowering=False, debug=False)
    NCH = n_chunks

    xin = nc.dram_tensor("x", [T, D], F32, kind="ExternalInput").ap()
    w_normw = nc.dram_tensor("norm_w", [D], F32, kind="ExternalInput").ap()
    w_normb = nc.dram_tensor("norm_b", [D], F32, kind="ExternalInput").ap()
    w_inproj = nc.dram_tensor("in_proj_w", [2 * DI, D], F32, kind="ExternalInput").ap()
    w_conv = nc.dram_tensor("conv_w", [DI, DI, 4], F32, kind="ExternalInput").ap()
    w_convb = nc.dram_tensor("conv_b", [DI], F32, kind="ExternalInput").ap()
    w_xproj = nc.dram_tensor("x_proj_w", [33, DI], F32, kind="ExternalInput").ap()
    w_dtw = nc.dram_tensor("dt_w", [DI, 1], F32, kind="ExternalInput").ap()
    w_dtb = nc.dram_tensor("dt_b", [DI], F32, kind="ExternalInput").ap()
    w_outproj = nc.dram_tensor("out_proj_w", [D, DI], F32, kind="ExternalInput").ap()
    w_gate = nc.dram_tensor("gate_w", [D, 8], F32, kind="ExternalInput").ap()
    w_shg = nc.dram_tensor("sh_gate", [D, DI], F32, kind="ExternalInput").ap()
    w_shu = nc.dram_tensor("sh_up", [D, DI], F32, kind="ExternalInput").ap()
    w_shd = nc.dram_tensor("sh_down", [DI, D], F32, kind="ExternalInput").ap()
    w_rtg = nc.dram_tensor("rt_gate", [8, D, DI], F32, kind="ExternalInput").ap()
    w_rtu = nc.dram_tensor("rt_up", [8, D, DI], F32, kind="ExternalInput").ap()
    w_rtd = nc.dram_tensor("rt_down", [8, DI, D], F32, kind="ExternalInput").ap()

    out = nc.dram_tensor("out", [T, D], F32, kind="ExternalOutput").ap()

    sc_wi = nc.dram_tensor("sc_wi", [32, 128, 8, 128], BF16, kind="Internal").ap()
    sc_cv = nc.dram_tensor("sc_cv", [16, 4, 128, 16, 128], BF16, kind="Internal").ap()
    sc_wo = nc.dram_tensor("sc_wo", [8, 128, 16, 128], BF16, kind="Internal").ap()
    sc_gu = nc.dram_tensor("sc_gu", [NEXP, 16, 128, 2, 8, 128], BF16, kind="Internal").ap()
    sc_dn = nc.dram_tensor("sc_dn", [NEXP, 8, 128, 16, 128], BF16, kind="Internal").ap()
    sc_ssm = nc.dram_tensor("sc_ssm", [2, 33, L], F32, kind="Internal").ap()
    sc_mrow = nc.dram_tensor("sc_mrow", [2, 8, L], F32, kind="Internal").ap()

    with ExitStack() as ctx:
        sb = lambda name, shape, dt: ctx.enter_context(nc.sbuf_tensor(name, shape, dt))
        ps = lambda name, shape, dt: ctx.enter_context(nc.psum_tensor(name, shape, dt))

        ident_f = sb("ident_f", [128, 128], F32)
        ident_b = sb("ident_b", [128, 128], BF16)
        ones128 = sb("ones128", [128, 128], F32)
        negdtw = sb("negdtw", [128, 16], F32)
        negdtb = sb("negdtb", [128, 16], F32)
        convb = sb("convb", [128, 16], F32)
        normw_fm = sb("normw_fm", [128, 8], F32)
        normb_fm = sb("normb_fm", [128, 8], F32)
        gatew_f32 = sb("gatew_f32", [128, 8 * 8], F32)
        xprojT_sb = sb("xprojT_sb", [128, 16 * 33], BF16)

        u16 = sb("u16", [128, 16 * 512], F32)
        y16 = sb("y16", [128, 16 * 512], F32)
        arnI = sb("arnI", [128, 4224], F32)
        arnJ = sb("arnJ", [128, 1024], F32)
        arnE = sb("arnE", [128, 8 * 512], BF16)
        xnT_bf = sb("xnT_bf", [128, 8 * 512], BF16)
        siluz_bf = sb("siluz_bf", [128, 16 * 512], BF16)
        arnD = sb("arnD", [128, 16 * 512], BF16)
        ffn_acc = sb("ffn_acc", [128, 8 * 512], F32)
        hist_bf = sb("hist_bf", [128, 16 * 4], BF16)
        hcar = sb("hcar", [128, 256], F32)
        dsrep = sb("dsrep", [128, 512], F32)
        brep = sb("brep", [128, 2 * 512], F32)
        crep = sb("crep", [128, 2 * 512], F32)
        bu2 = sb("bu2", [128, 2 * 512], F32)
        h2 = sb("h2", [128, 2 * 512], F32)
        a2 = sb("a2", [128, 4 * 512], F32)
        mrep = sb("mrep", [128, 2 * 512], BF16)
        xssmT = sb("xssmT", [33, 512], F32)
        lgT = xssmT[0:8, :]
        lgtok = sb("lgtok", [128, 32], F32)
        topv = sb("topv", [128, 4], F32)
        scoret = sb("scoret", [128, 4], F32)
        mtok = sb("mtok", [128, 32], F32)
        mrowsT = sb("mrowsT", [8, 512], F32)
        stats = sb("stats", [128, 16], F32)
        epsv = sb("epsv", [128, 1], F32)
        wa = sb("wa", [128, 8192], BF16)

        b0 = ps("b0", [128, 512], F32)
        b1 = ps("b1", [128, 512], F32)
        b2 = ps("b2", [128, 512], F32)
        b3 = ps("b3", [128, 512], F32)
        b4 = ps("b4", [128, 512], F32)
        b5 = ps("b5", [128, 512], F32)
        bt = ps("bt", [128, 1024], BF16)
        bx = ps("bx", [128, 512], F32)

        B = Bld(nc)
        B.mksems(ctx)

        arnI_bf = arnI[:, :].bitcast(BF16)
        arnJ_bf = arnJ[:, :].bitcast(BF16)

        def xin_slice(i):
            return arnI_bf[:, i * 520:(i + 1) * 520]

        def yz_slice(i):
            return arnI_bf[:, i * 520:i * 520 + 512]

        mark = {}

        B.op("gp", lambda e: e.memset(ones128[:, :], 1.0))
        B.op("gp", lambda e: e.affine_select(
            out=ident_f[:, :], in_=ones128[:, :], pattern=[[-1, 128]],
            compare_op=ALU.is_equal, fill=0.0, base=0, channel_multiplier=1))
        gpc_idf = B.cnt["GPC"]
        B.wait("dv", "GPC", gpc_idf)
        B.op("dv", lambda e: e.tensor_copy(out=ident_b[:, :], in_=ident_f[:, :]))
        B.op("dv", lambda e: e.memset(hcar[:, :], 0.0))
        B.op("dv", lambda e: e.memset(hist_bf[:, :], 0.0))
        B.op("dv", lambda e: e.memset(epsv[:, :], EPS))
        dv_idb = B.cnt["DV"]

        B.dma("gp", gatew_f32[:, :].rearrange("p (kt e) -> p kt e", kt=8),
              w_gate.rearrange("(kt p) e -> p kt e", p=128),
              allow_slow_non_contiguous=True)

        tmpv = arnJ[0:33, 0:128]
        last_vec_dv = 0
        last_vec_pe = 0

        def vec_tile(dst, src_ap, n_kt, scale):
            nonlocal last_vec_dv, last_vec_pe
            B.wait("gp", "PE", last_vec_pe)
            ld = B.dma("gp", tmpv[0:n_kt, :], src_ap)
            B.wait("pe", "GP", ld)
            B.wait("pe", "DV", max(dv_idb, last_vec_dv))
            B.wait("pe", "GPC", gpc_idf)
            pe_c = B.op("pe", lambda e, n_kt=n_kt: e.transpose(
                out=bx[:, 0:n_kt], in_=tmpv[0:n_kt, :],
                identity=ident_f[0:n_kt, 0:n_kt]))
            last_vec_pe = pe_c
            B.wait("dv", "PE", pe_c)
            B.op("dv", lambda e, dst=dst, n_kt=n_kt, scale=scale: e.tensor_scalar(
                out=dst[:, 0:n_kt], in0=bx[:, 0:n_kt],
                scalar1=float(scale), scalar2=None, op0=ALU.mult))
            last_vec_dv = B.cnt["DV"]
            return last_vec_dv

        vec_tile(negdtw, w_dtw.rearrange("(a b) one -> a (b one)", b=128), 16, -1.0)
        vec_tile(negdtb, w_dtb.rearrange("(a b) -> a b", b=128), 16, -1.0)
        vec_tile(convb, w_convb.rearrange("(a b) -> a b", b=128), 16, 1.0)
        vec_tile(normw_fm, w_normw.rearrange("(a b) -> a b", b=128), 8, 1.0)
        vec_tile(normb_fm, w_normb.rearrange("(a b) -> a b", b=128), 8, 1.0)

        xp_nat = arnD[0:33, 0:2048]
        ldxp = B.dma("gp", xp_nat, w_xproj)
        B.wait("pe", "GP", ldxp)
        for kt in range(16):
            bank = kt % 2
            pe_c = B.op("pe", lambda e, kt=kt, bank=bank: e.transpose(
                out=bt[:, bank * 512:bank * 512 + 33],
                in_=xp_nat[:, kt * 128:(kt + 1) * 128],
                identity=ident_b[0:33, 0:33]))
            B.wait("dv", "PE", pe_c)
            B.op("dv", lambda e, kt=kt, bank=bank: e.tensor_copy(
                out=xprojT_sb[:, kt * 33:(kt + 1) * 33],
                in_=bt[:, bank * 512:bank * 512 + 33]))
            B.wait("pe", "DV", B.cnt["DV"] - 1)
        dv_xp = B.cnt["DV"]

        slotA = wa[:, 0:8192]
        prepB = arnD[:, 4096:4608]
        stage_cv = arnD[:, 6144:8192]

        last_store = 0

        def prep_transposed():
            nonlocal last_store
            natv = slotA[:, 0:4096].rearrange("p (jt d) -> p jt d", jt=4)
            for jg in range(8):
                B.wait("gp", "SY", last_store)
                B.wait("gp", "PE", mark.get("prep_pe", 0))
                ld = B.dma("gp", natv,
                           w_inproj[jg * 512:(jg + 1) * 512, :].rearrange(
                               "(jt p) d -> p jt d", p=128))
                B.wait("pe", "GP", ld)
                for dkt in range(8):
                    bank_half = dkt % 2
                    bo = bank_half * 512
                    B.wait("pe", "DV", mark.get("prep_dv", 0))
                    for jt in range(4):
                        B.op("pe", lambda e, jt=jt, dkt=dkt, bo=bo: e.transpose(
                            out=bt[:, bo + jt * 128:bo + (jt + 1) * 128],
                            in_=natv[:, jt, dkt * 128:(dkt + 1) * 128],
                            identity=ident_b[:, :]))
                    pe_c = B.cnt["PE"]
                    mark["prep_pe"] = pe_c
                    B.wait("dv", "PE", pe_c)
                    B.wait("dv", "SY", last_store)
                    dv_c = B.op("dv", lambda e, bo=bo: e.tensor_copy(
                        out=prepB, in_=bt[:, bo:bo + 512]))
                    mark["prep_dv"] = dv_c
                    B.wait("sy", "DV", dv_c)
                    mb = jg * 4
                    last_store = B.dma(
                        "sy", sc_wi[mb:mb + 4, :, dkt, :].rearrange("m p f -> p m f"),
                        prepB.rearrange("p (m f) -> p m f", m=4))
            natc = slotA[:, 0:8192]
            natc_v = natc.rearrange("p (i k) -> p k i", k=4)
            for mt in range(16):
                B.wait("gp", "SY", last_store)
                B.wait("gp", "PE", mark.get("prep_pe", 0))
                ld = B.dma("gp", natc,
                           w_conv[mt * 128:(mt + 1) * 128, :, :].rearrange("p i k -> p (i k)"))
                B.wait("pe", "GP", ld)
                for k in range(4):
                    for ikg in range(4):
                        bo = (ikg % 2) * 512
                        B.wait("pe", "DV", mark.get("prep_dv", 0))
                        for q in range(4):
                            ikt = ikg * 4 + q
                            B.op("pe", lambda e, k=k, ikt=ikt, q=q, bo=bo: e.transpose(
                                out=bt[:, bo + q * 128:bo + (q + 1) * 128],
                                in_=natc_v[:, k, ikt * 128:(ikt + 1) * 128],
                                identity=ident_b[:, :]))
                        pe_c = B.cnt["PE"]
                        mark["prep_pe"] = pe_c
                        B.wait("dv", "PE", pe_c)
                        B.wait("dv", "SY", last_store)
                        dv_c = B.op("dv", lambda e, ikg=ikg, bo=bo: e.tensor_copy(
                            out=stage_cv[:, ikg * 512:(ikg + 1) * 512],
                            in_=bt[:, bo:bo + 512]))
                        mark["prep_dv"] = dv_c
                    dv_c = B.cnt["DV"]
                    B.wait("sy", "DV", dv_c)
                    last_store = B.dma(
                        "sy", sc_cv[mt, k, :, :, :].rearrange("p i f -> p (i f)"),
                        stage_cv)
            nato = slotA[:, 0:8192].rearrange("p (dt j) -> p dt j", dt=4)
            for dg in range(2):
                B.wait("gp", "SY", last_store)
                B.wait("gp", "PE", mark.get("prep_pe", 0))
                ld = B.dma("gp", nato,
                           w_outproj[dg * 512:(dg + 1) * 512, :].rearrange(
                               "(dt p) j -> p dt j", p=128))
                B.wait("pe", "GP", ld)
                for jkt in range(16):
                    bo = (jkt % 2) * 512
                    B.wait("pe", "DV", mark.get("prep_dv", 0))
                    for dt in range(4):
                        B.op("pe", lambda e, dt=dt, jkt=jkt, bo=bo: e.transpose(
                            out=bt[:, bo + dt * 128:bo + (dt + 1) * 128],
                            in_=nato[:, dt, jkt * 128:(jkt + 1) * 128],
                            identity=ident_b[:, :]))
                    pe_c = B.cnt["PE"]
                    mark["prep_pe"] = pe_c
                    B.wait("dv", "PE", pe_c)
                    B.wait("dv", "SY", last_store)
                    dv_c = B.op("dv", lambda e, bo=bo: e.tensor_copy(
                        out=prepB, in_=bt[:, bo:bo + 512]))
                    mark["prep_dv"] = dv_c
                    B.wait("sy", "DV", dv_c)
                    last_store = B.dma(
                        "sy", sc_wo[dg * 4:(dg + 1) * 4, :, jkt, :].rearrange("m p f -> p m f"),
                        prepB.rearrange("p (m f) -> p m f", m=4))

        def prep_moe():
            nonlocal last_store
            srcs_gu = [(w_rtg[e2], w_rtu[e2]) for e2 in range(8)] + [(w_shg, w_shu)]
            srcs_dn = [w_rtd[e2] for e2 in range(8)] + [w_shd]
            stgs = [slotA[:, 0:2048], slotA[:, 2048:4096]]
            last_use = [last_store, last_store]
            par = 0
            for em in range(NEXP):
                g_src, u_src = srcs_gu[em]
                for gu in range(2):
                    src = (g_src, u_src)[gu]
                    for kt in range(8):
                        stg = stgs[par]
                        B.wait("gp", "SY", last_use[par])
                        ld = B.dma("gp", stg, src[kt * 128:(kt + 1) * 128, :])
                        B.wait("sy", "GP", ld)
                        last_use[par] = B.dma(
                            "sy", sc_gu[em, :, :, gu, kt, :].rearrange("m p f -> p m f"),
                            stg.rearrange("p (m f) -> p m f", m=16))
                        par ^= 1
                for kt in range(16):
                    stg = stgs[par][:, 0:1024]
                    B.wait("gp", "SY", last_use[par])
                    ld = B.dma("gp", stg, srcs_dn[em][kt * 128:(kt + 1) * 128, :])
                    B.wait("sy", "GP", ld)
                    last_use[par] = B.dma(
                        "sy", sc_dn[em, :, :, kt, :].rearrange("m p f -> p m f"),
                        stg.rearrange("p (m f) -> p m f", m=8))
                    par ^= 1
            last_store = max(last_use)

        prep_transposed()
        prep_moe()

        prep_sy, prep_gp, prep_dv = B.cnt["SY"], B.cnt["GP"], B.cnt["DV"]
        for eng in ("pe", "dv", "ac", "sy", "gp"):
            B.wait(eng, "SY", prep_sy)
            B.wait(eng, "GP", prep_gp)
            B.wait(eng, "DV", prep_dv)

        bank_mark = {}

        def bank_wait(eng, bank):
            s, c = bank_mark.get(id(bank), ("DV", 0))
            B.wait(eng, s, c)

        def bank_consumed(bank, sem, cnt):
            bank_mark[id(bank)] = (sem, cnt)

        wslots = [wa[:, i * 2048:(i + 1) * 2048] for i in range(4)]
        wslot_mark = [0] * 4
        wslot_load = [0] * 4
        wrr = [0]

        def wload(src_ap, ncols):
            i = wrr[0] % 4
            wrr[0] += 1
            B.wait("sy", "PE", wslot_mark[i])
            wslot_load[i] = B.dma("sy", wslots[i][:, 0:ncols], src_ap)
            return i

        def wuse(i):
            B.wait("pe", "SY", wslot_load[i])

        def wdone(i):
            wslot_mark[i] = B.cnt["PE"]

        def tok_ln(dst, xslot, xmm, sq, mu_col, var_col, rstd_col):
            B.op("ac", lambda e: e.activation(out=dst, in_=xslot, func=AF.Copy,
                                              accum_out=stats[:, mu_col:mu_col + 1]))
            acc = B.cnt["AC"]
            B.wait("ac", "AC", acc)
            B.op("ac", lambda e: e.activation(out=stats[:, mu_col + 1:mu_col + 2],
                                              in_=stats[:, mu_col:mu_col + 1],
                                              func=AF.Copy, scale=-1.0 / D))
            acc = B.cnt["AC"]
            B.wait("ac", "AC", acc)
            B.op("ac", lambda e: e.activation(out=xmm, in_=xslot, func=AF.Identity,
                                              bias=stats[:, mu_col + 1:mu_col + 2]))
            B.op("ac", lambda e: e.activation(out=sq, in_=xmm, func=AF.Square,
                                              accum_out=stats[:, var_col:var_col + 1]))
            acc = B.cnt["AC"]
            B.wait("ac", "AC", acc)
            B.op("ac", lambda e: e.activation(out=stats[:, var_col + 1:var_col + 2],
                                              in_=stats[:, var_col:var_col + 1],
                                              func=AF.Sqrt, bias=epsv[:, 0:1],
                                              scale=1.0 / D))
            acc = B.cnt["AC"]
            B.wait("dv", "AC", acc)
            dvr = B.op("dv", lambda e: e.reciprocal(out=stats[:, rstd_col:rstd_col + 1],
                                                    in_=stats[:, var_col + 1:var_col + 2]))
            B.wait("ac", "DV", dvr)
            B.op("ac", lambda e: e.activation(out=dst, in_=xmm, func=AF.Identity,
                                              scale=stats[:, rstd_col:rstd_col + 1]))
            return B.cnt["AC"]

        def ln_zone(c):
            xslot = arnI[:, 0:1024]
            xmm = arnI[:, 1024:2048]
            xn_t = arnI[:, 2048:3072]
            raw2 = arnI[:, 3072:4096]
            sq = arnJ[:, 0:1024]
            xnbf_st = arnJ_bf[:, 0:1024]
            r2bf_st = arnJ_bf[:, 1024:2048]
            xfree = 0
            for tt in range(NTT):
                B.wait("gp", "SY", mark.get("out_store", 0))
                B.wait("gp", "DV", mark.get("merge_x_used", 0))
                B.wait("gp", "AC", xfree)
                ld = B.dma("gp", xslot,
                           xin[c * L + tt * 128: c * L + (tt + 1) * 128, :])
                B.wait("ac", "GP", ld)
                B.wait("ac", "PE", mark.get("pe_gate", 0))
                B.wait("ac", "SY", mark.get("out_store", 0))
                tok_ln(xn_t, xslot, xmm, sq, 0, 2, 4)
                tok_ln(raw2, xn_t, xmm, sq, 5, 7, 9)
                B.op("ac", lambda e: e.activation(out=xnbf_st, in_=xn_t, func=AF.Copy))
                dv_xnbf = B.cnt["AC"]
                B.op("ac", lambda e: e.activation(out=r2bf_st, in_=raw2, func=AF.Copy))
                dv_r2bf = B.cnt["AC"]
                mark["lg_tmpf_free"] = B.cnt["DV"]
                for src_st, ac_src, affine in ((xnbf_st, dv_xnbf, False),
                                               (r2bf_st, dv_r2bf, True)):
                    srcv = src_st.rearrange("p (dt f) -> p dt f", dt=8)
                    for half in range(2):
                        bank_wait("pe", bt)
                        B.wait("pe", "AC", ac_src)
                        for g in range(4):
                            dt = half * 4 + g
                            B.op("pe", lambda e, dt=dt, g=g, srcv=srcv: e.transpose(
                                out=bt[:, g * 128:(g + 1) * 128],
                                in_=srcv[:, dt, :], identity=ident_b[:, :]))
                        pe_c = B.cnt["PE"]
                        B.wait("ac", "PE", pe_c)
                        for g in range(4):
                            dt = half * 4 + g
                            col = dt * 512 + tt * 128
                            if affine:
                                B.op("ac", lambda e, dt=dt, g=g, col=col: e.activation(
                                    out=arnE[:, col:col + 128],
                                    in_=bt[:, g * 128:(g + 1) * 128], func=AF.Identity,
                                    scale=normw_fm[:, dt:dt + 1],
                                    bias=normb_fm[:, dt:dt + 1]))
                            else:
                                B.op("ac", lambda e, g=g, col=col: e.activation(
                                    out=xnT_bf[:, col:col + 128],
                                    in_=bt[:, g * 128:(g + 1) * 128], func=AF.Copy))
                        bank_consumed(bt, "AC", B.cnt["AC"])
                tmpf = xmm
                ac_c = 0
                for half in range(2):
                    bank_wait("pe", bx)
                    for g in range(4):
                        dt = half * 4 + g
                        B.op("pe", lambda e, dt=dt, g=g: e.transpose(
                            out=bx[:, g * 128:(g + 1) * 128],
                            in_=xn_t[:, dt * 128:(dt + 1) * 128],
                            identity=ident_f[:, :]))
                    pe_c = B.cnt["PE"]
                    B.wait("ac", "PE", pe_c)
                    B.wait("ac", "DV", mark.get("lg_tmpf_free", 0))
                    ac_c = B.op("ac", lambda e, half=half: e.activation(
                        out=tmpf[:, half * 512:(half + 1) * 512], in_=bx[:, 0:512],
                        func=AF.Copy))
                    bank_consumed(bx, "AC", ac_c)
                B.wait("pe", "AC", ac_c)
                bank_wait("pe", b2)
                for dt in range(8):
                    B.op("pe", lambda e, dt=dt, tt=tt: e.matmul(
                        out=b2[0:8, tt * 128:(tt + 1) * 128],
                        lhsT=gatew_f32[:, dt * 8:(dt + 1) * 8],
                        rhs=tmpf[:, dt * 128:(dt + 1) * 128],
                        start=(dt == 0), stop=(dt == 7)))
                pe_gate = B.cnt["PE"]
                mark["pe_gate"] = pe_gate
                B.wait("ac", "PE", pe_gate)
                B.op("ac", lambda e, tt=tt: e.activation(
                    out=lgT[:, tt * 128:(tt + 1) * 128],
                    in_=b2[0:8, tt * 128:(tt + 1) * 128], func=AF.Copy))
                bank_consumed(b2, "AC", B.cnt["AC"])
                xfree = B.cnt["AC"]
            mark["dv_lnT"] = B.cnt["DV"]
            mark["ac_lnT"] = B.cnt["AC"]

        def mask_zone(c):
            bank_wait("pe", bx)
            B.wait("pe", "AC", mark["ac_lnT"])
            for tt in range(NTT):
                B.op("pe", lambda e, tt=tt: e.transpose(
                    out=bx[:, tt * 8:(tt + 1) * 8],
                    in_=lgT[:, tt * 128:(tt + 1) * 128],
                    identity=ident_f[0:8, 0:8]))
            pe_tp = B.cnt["PE"]
            B.wait("ac", "PE", pe_tp)
            ac_lgtok = B.op("ac", lambda e: e.activation(out=lgtok[:, :], in_=bx[:, 0:32],
                                                         func=AF.Copy))
            bank_consumed(bx, "AC", ac_lgtok)
            B.wait("dv", "AC", ac_lgtok)
            for tt in range(NTT):
                B.op("dv", lambda e, tt=tt: e.reduce_max(
                    out=topv[:, tt:tt + 1], in_=lgtok[:, tt * 8:(tt + 1) * 8], axis=AX.X))
            dv_top = B.cnt["DV"]
            B.wait("ac", "DV", dv_top)
            B.op("ac", lambda e: e.activation(out=scoret[:, 0:4], in_=topv[:, 0:4],
                                              func=AF.Sigmoid))
            ac_sc = B.cnt["AC"]
            B.wait("dv", "AC", ac_sc)
            for tt in range(NTT):
                B.op("dv", lambda e, tt=tt: e.tensor_scalar(
                    out=mtok[:, tt * 8:(tt + 1) * 8], in0=lgtok[:, tt * 8:(tt + 1) * 8],
                    scalar1=topv[:, tt:tt + 1], scalar2=scoret[:, tt:tt + 1],
                    op0=ALU.is_equal, op1=ALU.mult))
            dv_mtok = B.cnt["DV"]
            bank_wait("pe", bx)
            B.wait("pe", "DV", dv_mtok)
            for tt in range(NTT):
                B.op("pe", lambda e, tt=tt: e.transpose(
                    out=bx[0:8, tt * 128:(tt + 1) * 128], in_=mtok[:, tt * 8:(tt + 1) * 8],
                    identity=ident_f[:, :]))
            pe_mr = B.cnt["PE"]
            B.wait("ac", "PE", pe_mr)
            dv_mr = B.op("ac", lambda e: e.activation(out=mrowsT[:, :], in_=bx[0:8, 0:512],
                                                      func=AF.Copy))
            bank_consumed(bx, "AC", dv_mr)
            B.wait("gp", "AC", dv_mr)
            B.wait("gp", "GP", mark.get(("mrep_used", c % 2), 0))
            mark[("mrow_store", c % 2)] = B.dma("gp", sc_mrow[c % 2], mrowsT[:, :])

        def in_proj(c):
            B.wait("dv", "SY", mark.get("out_store", 0))
            for i in range(16):
                B.op("dv", lambda e, i=i: e.tensor_copy(
                    out=xin_slice(i)[:, 0:3], in_=hist_bf[:, i * 4:i * 4 + 3]))
            dv_hist = B.cnt["DV"]
            banks = (b0, b1)
            for m in range(32):
                si = wload(sc_wi[m].rearrange("p kt f -> p (kt f)"), 1024)
                bk = banks[m % 2]
                wuse(si)
                bank_wait("pe", bk)
                B.wait("pe", "AC", mark["ac_lnT"])
                for kt in range(8):
                    B.op("pe", lambda e, kt=kt, si=si, bk=bk: e.matmul(
                        out=bk[:, :], lhsT=wslots[si][:, kt * 128:(kt + 1) * 128],
                        rhs=arnE[:, kt * 512:(kt + 1) * 512],
                        start=(kt == 0), stop=(kt == 7)))
                wdone(si)
                pe_c = B.cnt["PE"]
                if m < 16:
                    B.wait("ac", "PE", pe_c)
                    B.wait("ac", "DV", dv_hist)
                    B.op("ac", lambda e, m=m, bk=bk: e.activation(
                        out=xin_slice(m)[:, 3:515], in_=bk[:, :], func=AF.Copy))
                    bank_consumed(bk, "AC", B.cnt["AC"])
                else:
                    B.wait("ac", "PE", pe_c)
                    B.wait("ac", "DV", mark.get("dv_yz", 0))
                    B.op("ac", lambda e, m=m, bk=bk: e.activation(
                        out=siluz_bf[:, (m - 16) * 512:(m - 15) * 512],
                        in_=bk[:, :], func=AF.Silu))
                    bank_consumed(bk, "AC", B.cnt["AC"])
            mark["dv_xin"] = B.cnt["DV"]
            mark["ac_xin"] = B.cnt["AC"]

        def conv(c):
            banks = (b0, b1)
            for m in range(16):
                slots = [wload(sc_cv[m, k].rearrange("p i f -> p (i f)"), 2048)
                         for k in range(4)]
                bk = banks[m % 2]
                bank_wait("pe", bk)
                B.wait("pe", "AC", mark["ac_xin"])
                first = True
                for k in range(4):
                    si = slots[k]
                    wuse(si)
                    for ikt in range(16):
                        B.op("pe", lambda e, k=k, ikt=ikt, si=si, bk=bk, first=first: e.matmul(
                            out=bk[:, :], lhsT=wslots[si][:, ikt * 128:(ikt + 1) * 128],
                            rhs=xin_slice(ikt)[:, k:k + 512],
                            start=first, stop=(k == 3 and ikt == 15)))
                        first = False
                    wdone(si)
                pe_c = B.cnt["PE"]
                B.wait("ac", "PE", pe_c)
                B.wait("ac", "DV", mark.get("dv_scan_u_free", 0))
                B.op("ac", lambda e, m=m, bk=bk: e.activation(
                    out=u16[:, m * 512:(m + 1) * 512], in_=bk[:, :], func=AF.Silu,
                    bias=convb[:, m:m + 1]))
                ac_c = B.cnt["AC"]
                bank_consumed(bk, "AC", ac_c)
                B.op("ac", lambda e, m=m: e.activation(
                    out=arnD[:, m * 512:(m + 1) * 512],
                    in_=u16[:, m * 512:(m + 1) * 512], func=AF.Copy))
            ac_all = B.cnt["AC"]
            B.wait("dv", "AC", ac_all)
            for i in range(16):
                B.op("dv", lambda e, i=i: e.tensor_copy(
                    out=hist_bf[:, i * 4:i * 4 + 3], in_=xin_slice(i)[:, 512:515]))
            mark["ac_xact"] = ac_all
            mark["ac_u"] = ac_all

        def x_proj(c):
            bank_wait("pe", b2)
            B.wait("pe", "AC", mark["ac_xact"])
            for kt in range(16):
                B.op("pe", lambda e, kt=kt: e.matmul(
                    out=b2[0:33, :], lhsT=xprojT_sb[:, kt * 33:(kt + 1) * 33],
                    rhs=arnD[:, kt * 512:(kt + 1) * 512],
                    start=(kt == 0), stop=(kt == 15)))
            pe_c = B.cnt["PE"]
            B.wait("ac", "PE", pe_c)
            dv_c = B.op("ac", lambda e: e.activation(out=xssmT[:, :], in_=b2[0:33, :],
                                                     func=AF.Copy))
            bank_consumed(b2, "AC", dv_c)
            B.wait("gp", "AC", dv_c)
            B.wait("gp", "GP", mark.get(("ssm_used", c % 2), 0))
            st = B.dma("gp", sc_ssm[c % 2], xssmT[:, :])
            B.wait("gp", "GP", st)
            B.wait("gp", "AC", mark.get("ac_scan", 0))
            B.dma("gp", dsrep[:, :], sc_ssm[c % 2, 0:1, :].partition_broadcast(128))
            B.dma("gp", brep[:, 0:512], sc_ssm[c % 2, 1:2, :].partition_broadcast(128))
            mark["gp_rep0"] = B.dma(
                "gp", crep[:, 0:512], sc_ssm[c % 2, 17:18, :].partition_broadcast(128))

        def scan_ffn(c):
            prev = mark.get("dv_y", 0)
            dv_used_par = [prev, prev]
            a_used_par = [prev, prev, prev, prev]
            prod_used = [mark.get("gpc_y", 0), mark.get("gpc_y", 0)]
            rep_ready = {0: mark["gp_rep0"]}
            B.wait("dv", "PE", mark.get("pe_merge", 0))
            B.wait("gp", "PE", mark.get("pe_merge", 0))
            prevf = mark.get("dv_ffn", 0)
            mrep_used = [prevf, prevf]
            xmT_pe_used = [0]
            emitted = [0]

            def emit_expert():
                em = emitted[0]
                emitted[0] += 1
                epar = em % 2
                if em < 8:
                    B.wait("gp", "GP", mark[("mrow_store", c % 2)])
                    B.wait("gp", "DV", mrep_used[epar])
                    ld = B.dma("gp", mrep[:, epar * 512:(epar + 1) * 512],
                               sc_mrow[c % 2, em:em + 1, :].partition_broadcast(128))
                    B.wait("dv", "GP", ld)
                    B.wait("dv", "PE", xmT_pe_used[0])
                    for dt in range(8):
                        B.op("dv", lambda e, dt=dt, epar=epar: e.tensor_tensor(
                            out=arnE[:, dt * 512:(dt + 1) * 512],
                            in0=xnT_bf[:, dt * 512:(dt + 1) * 512],
                            in1=mrep[:, epar * 512:(epar + 1) * 512], op=ALU.mult))
                    mrep_used[epar] = B.cnt["DV"]
                    dv_xmT = B.cnt["DV"]
                    rhs_buf = arnE
                else:
                    dv_xmT = 0
                    rhs_buf = xnT_bf
                for m in range(16):
                    si = wload(sc_gu[em, m].rearrange("p gu kt f -> p (gu kt f)"), 2048)
                    wuse(si)
                    bg, bu_ = (b0, b1) if m % 2 == 0 else (b2, b3)
                    bank_wait("pe", bg)
                    bank_wait("pe", bu_)
                    B.wait("pe", "DV", dv_xmT)
                    for kt in range(8):
                        B.op("pe", lambda e, si=si, kt=kt, rb=rhs_buf, bg=bg: e.matmul(
                            out=bg[:, :], lhsT=wslots[si][:, kt * 128:(kt + 1) * 128],
                            rhs=rb[:, kt * 512:(kt + 1) * 512],
                            start=(kt == 0), stop=(kt == 7)))
                    for kt in range(8):
                        B.op("pe", lambda e, si=si, kt=kt, rb=rhs_buf, bu_=bu_: e.matmul(
                            out=bu_[:, :],
                            lhsT=wslots[si][:, 1024 + kt * 128:1024 + (kt + 1) * 128],
                            rhs=rb[:, kt * 512:(kt + 1) * 512],
                            start=(kt == 0), stop=(kt == 7)))
                    wdone(si)
                    pe_c = B.cnt["PE"]
                    jpar = m % 2
                    B.wait("ac", "PE", pe_c)
                    B.wait("ac", "DV", mark.get(("silug_used", jpar), 0))
                    B.op("ac", lambda e, jpar=jpar, bg=bg: e.activation(
                        out=arnJ[:, jpar * 512:(jpar + 1) * 512], in_=bg[:, :],
                        func=AF.Silu))
                    ac_c = B.cnt["AC"]
                    bank_consumed(bg, "AC", ac_c)
                    B.wait("dv", "AC", ac_c)
                    B.op("dv", lambda e, m=m, jpar=jpar, bu_=bu_: e.tensor_tensor(
                        out=arnD[:, m * 512:(m + 1) * 512],
                        in0=arnJ[:, jpar * 512:(jpar + 1) * 512], in1=bu_[:, :],
                        op=ALU.mult))
                    dv_c = B.cnt["DV"]
                    mark[("silug_used", jpar)] = dv_c
                    bank_consumed(bu_, "DV", dv_c)
                xmT_pe_used[0] = B.cnt["PE"]
                dv_act = B.cnt["DV"]
                banks = (b4, b5)
                for m2 in range(8):
                    si = wload(sc_dn[em, m2].rearrange("p kt f -> p (kt f)"), 2048)
                    bk = banks[m2 % 2]
                    wuse(si)
                    bank_wait("pe", bk)
                    B.wait("pe", "DV", dv_act)
                    for kt in range(16):
                        B.op("pe", lambda e, si=si, kt=kt, bk=bk: e.matmul(
                            out=bk[:, :], lhsT=wslots[si][:, kt * 128:(kt + 1) * 128],
                            rhs=arnD[:, kt * 512:(kt + 1) * 512],
                            start=(kt == 0), stop=(kt == 15)))
                    wdone(si)
                    pe_c = B.cnt["PE"]
                    B.wait("dv", "PE", pe_c)
                    if em == 0:
                        B.op("dv", lambda e, m2=m2, bk=bk: e.tensor_copy(
                            out=ffn_acc[:, m2 * 512:(m2 + 1) * 512], in_=bk[:, :]))
                    else:
                        B.op("dv", lambda e, m2=m2, bk=bk: e.tensor_tensor(
                            out=ffn_acc[:, m2 * 512:(m2 + 1) * 512],
                            in0=ffn_acc[:, m2 * 512:(m2 + 1) * 512], in1=bk[:, :],
                            op=ALU.add))
                    bank_consumed(bk, "DV", B.cnt["DV"])

            for s in range(DS):
                par = s % 2
                if s + 1 < DS:
                    B.wait("gp", "DV", dv_used_par[1 - par])
                    B.dma("gp", brep[:, (1 - par) * 512:(2 - par) * 512],
                          sc_ssm[c % 2, 2 + s:3 + s, :].partition_broadcast(128))
                    rep_ready[s + 1] = B.dma(
                        "gp", crep[:, (1 - par) * 512:(2 - par) * 512],
                        sc_ssm[c % 2, 18 + s:19 + s, :].partition_broadcast(128))
                B.wait("dv", "GP", rep_ready[s])
                for dt in range(16):
                    apar = dt % 4
                    B.wait("ac", "DV", a_used_par[apar])
                    if s == 0 and dt == 0:
                        B.wait("ac", "GP", mark["gp_rep0"])
                    B.op("ac", lambda e, dt=dt, apar=apar: e.activation(
                        out=a2[:, apar * 512:(apar + 1) * 512], in_=dsrep[:, :],
                        func=AF.Sigmoid, scale=negdtw[:, dt:dt + 1],
                        bias=negdtb[:, dt:dt + 1]))
                    ac_a = B.cnt["AC"]
                    bpar = dt % 2
                    B.wait("dv", "AC", mark["ac_u"])
                    B.wait("dv", "GPC", prod_used[bpar])
                    B.op("dv", lambda e, dt=dt, par=par, bpar=bpar: e.tensor_tensor(
                        out=bu2[:, bpar * 512:(bpar + 1) * 512],
                        in0=u16[:, dt * 512:(dt + 1) * 512],
                        in1=brep[:, par * 512:(par + 1) * 512], op=ALU.mult))
                    B.wait("dv", "AC", ac_a)
                    idx = s * 16 + dt
                    hpar = dt % 2
                    B.op("dv", lambda e, apar=apar, bpar=bpar, hpar=hpar, idx=idx:
                         e.tensor_tensor_scan(
                             out=h2[:, hpar * 512:(hpar + 1) * 512],
                             data0=a2[:, apar * 512:(apar + 1) * 512],
                             data1=bu2[:, bpar * 512:(bpar + 1) * 512],
                             initial=hcar[:, idx:idx + 1], op0=ALU.mult, op1=ALU.add))
                    a_used_par[apar] = B.cnt["DV"]
                    if s == 0:
                        B.op("dv", lambda e, dt=dt, hpar=hpar, par=par: e.tensor_tensor(
                            out=y16[:, dt * 512:(dt + 1) * 512],
                            in0=h2[:, hpar * 512:(hpar + 1) * 512],
                            in1=crep[:, par * 512:(par + 1) * 512], op=ALU.mult))
                        B.op("dv", lambda e, hpar=hpar, idx=idx: e.tensor_copy(
                            out=hcar[:, idx:idx + 1],
                            in_=h2[:, hpar * 512 + 511: hpar * 512 + 512]))
                    else:
                        dv_p = B.op("dv", lambda e, hpar=hpar, par=par, bpar=bpar: e.tensor_tensor(
                            out=bu2[:, bpar * 512:(bpar + 1) * 512],
                            in0=h2[:, hpar * 512:(hpar + 1) * 512],
                            in1=crep[:, par * 512:(par + 1) * 512], op=ALU.mult))
                        B.op("dv", lambda e, hpar=hpar, idx=idx: e.tensor_copy(
                            out=hcar[:, idx:idx + 1],
                            in_=h2[:, hpar * 512 + 511: hpar * 512 + 512]))
                        B.wait("gp", "DV", dv_p)
                        gadd = B.op("gp", lambda e, dt=dt, bpar=bpar: e.tensor_tensor(
                            out=y16[:, dt * 512:(dt + 1) * 512],
                            in0=y16[:, dt * 512:(dt + 1) * 512],
                            in1=bu2[:, bpar * 512:(bpar + 1) * 512], op=ALU.add))
                        prod_used[bpar] = gadd
                dv_used_par[par] = B.cnt["DV"]
                want = min(NEXP, ((s + 1) * NEXP + DS - 1) // DS)
                while emitted[0] < want:
                    emit_expert()
            while emitted[0] < NEXP:
                emit_expert()
            mark[("ssm_used", c % 2)] = B.cnt["GP"]
            mark["dv_y"] = B.cnt["DV"]
            mark["ac_scan"] = B.cnt["AC"]
            mark["dv_scan_u_free"] = B.cnt["DV"]
            mark[("mrep_used", c % 2)] = B.cnt["GP"]
            mark["dv_ffn"] = B.cnt["DV"]
            mark["gpc_y"] = B.cnt["GPC"]

        def yz(c):
            B.wait("dv", "GPC", mark.get("gpc_y", 0))
            for m in range(16):
                B.op("dv", lambda e, m=m: e.tensor_tensor(
                    out=yz_slice(m), in0=y16[:, m * 512:(m + 1) * 512],
                    in1=siluz_bf[:, m * 512:(m + 1) * 512], op=ALU.mult))
            mark["dv_yz"] = B.cnt["DV"]

        def out_proj(c):
            banks = (b0, b1)
            for m in range(8):
                si = wload(sc_wo[m].rearrange("p kt f -> p (kt f)"), 2048)
                bk = banks[m % 2]
                wuse(si)
                bank_wait("pe", bk)
                B.wait("pe", "DV", mark["dv_yz"])
                for kt in range(16):
                    B.op("pe", lambda e, kt=kt, si=si, bk=bk: e.matmul(
                        out=bk[:, :], lhsT=wslots[si][:, kt * 128:(kt + 1) * 128],
                        rhs=yz_slice(kt), start=(kt == 0), stop=(kt == 15)))
                wdone(si)
                pe_c = B.cnt["PE"]
                B.wait("ac", "PE", pe_c)
                B.op("ac", lambda e, m=m, bk=bk: e.activation(
                    out=y16[:, m * 512:(m + 1) * 512], in_=bk[:, :], func=AF.Copy))
                bank_consumed(bk, "AC", B.cnt["AC"])
            mark["ac_opT"] = B.cnt["AC"]
            mark["pe_op"] = B.cnt["PE"]

        def merge(c):
            xslot = arnI[:, 0:1024]
            xmm = arnI[:, 1024:2048]
            xn_t = arnI[:, 2048:3072]
            mgG = arnI[:, 3072:4096]
            sq = arnJ[:, 0:1024]
            last_st = mark.get("out_store", 0)
            for tt in range(NTT):
                B.wait("gp", "PE", mark["pe_op"])
                B.wait("gp", "DV", mark.get("merge_x_used", 0))
                B.wait("gp", "SY", last_st)
                ld = B.dma("gp", xslot,
                           xin[c * L + tt * 128: c * L + (tt + 1) * 128, :])
                B.wait("ac", "GP", ld)
                B.wait("ac", "SY", last_st)
                B.wait("ac", "PE", mark["pe_op"])
                B.wait("ac", "DV", mark.get("merge_x_used", 0))
                ac_xn = tok_ln(xn_t, xslot, xmm, sq, 0, 2, 4)
                for half in range(2):
                    bank_wait("pe", bx)
                    B.wait("pe", "AC", max(mark["ac_opT"], ac_xn))
                    for g in range(4):
                        dt = half * 4 + g
                        B.op("pe", lambda e, dt=dt, g=g, tt=tt: e.transpose(
                            out=bx[:, g * 128:(g + 1) * 128],
                            in_=y16[:, dt * 512 + tt * 128: dt * 512 + (tt + 1) * 128],
                            identity=ident_f[:, :]))
                    pe_c = B.cnt["PE"]
                    B.wait("dv", "PE", pe_c)
                    B.wait("dv", "AC", ac_xn)
                    B.op("dv", lambda e, half=half: e.tensor_tensor(
                        out=xmm[:, half * 512:(half + 1) * 512], in0=bx[:, 0:512],
                        in1=xn_t[:, half * 512:(half + 1) * 512], op=ALU.add))
                    bank_consumed(bx, "DV", B.cnt["DV"])
                for half in range(2):
                    bank_wait("pe", bx)
                    B.wait("pe", "DV", mark["dv_ffn"])
                    for g in range(4):
                        m2 = half * 4 + g
                        B.op("pe", lambda e, m2=m2, g=g, tt=tt: e.transpose(
                            out=bx[:, g * 128:(g + 1) * 128],
                            in_=ffn_acc[:, m2 * 512 + tt * 128: m2 * 512 + (tt + 1) * 128],
                            identity=ident_f[:, :]))
                    pe_c = B.cnt["PE"]
                    B.wait("ac", "PE", pe_c)
                    B.wait("ac", "DV", mark.get("merge_g_used", 0))
                    B.op("ac", lambda e, half=half: e.activation(
                        out=mgG[:, half * 512:(half + 1) * 512], in_=bx[:, 0:512],
                        func=AF.Silu))
                    bank_consumed(bx, "AC", B.cnt["AC"])
                ac_g = B.cnt["AC"]
                B.wait("dv", "AC", ac_g)
                B.op("dv", lambda e: e.tensor_tensor(out=xmm, in0=xmm, in1=mgG,
                                                     op=ALU.mult))
                B.op("dv", lambda e: e.tensor_tensor(out=mgG, in0=xmm, in1=xslot,
                                                     op=ALU.add))
                dv_out = B.cnt["DV"]
                mark["merge_x_used"] = dv_out
                mark["merge_g_used"] = dv_out
                B.wait("sy", "DV", dv_out)
                last_st = B.dma("sy", out[c * L + tt * 128: c * L + (tt + 1) * 128, :],
                                mgG)
            mark["out_store"] = last_st
            mark["pe_merge"] = B.cnt["PE"]

        def do_tap(names_bufs):
            stores = 0
            for name, buf, shape, dt in names_bufs:
                tout = nc.dram_tensor("tap_" + name, shape, dt,
                                      kind="ExternalOutput").ap()
                B.wait("sy", "DV", B.cnt["DV"])
                B.wait("sy", "AC", B.cnt["AC"])
                B.wait("sy", "PE", B.cnt["PE"])
                B.wait("sy", "GP", B.cnt["GP"])
                stores = B.dma("sy", tout, buf)
            B.wait("sy", "SY", stores)

        done = False
        for c in range(NCH):
            ln_zone(c)
            if tap == "ln" and c == tap_chunk:
                do_tap([("xnT", xnT_bf[:, :], [128, 8 * 512], BF16),
                        ("ln2T", arnE[:, :], [128, 8 * 512], BF16)])
                done = True
                break
            mask_zone(c)
            if tap == "mask" and c == tap_chunk:
                do_tap([("mtok", mtok[:, :], [128, 32], F32),
                        ("mrowsT", mrowsT[:, :], [8, 512], F32)])
                done = True
                break
            in_proj(c)
            if tap == "inproj" and c == tap_chunk:
                do_tap([("xinT", arnI_bf[:, 0:8320], [128, 8320], BF16),
                        ("siluz", siluz_bf[:, :], [128, 16 * 512], BF16)])
                done = True
                break
            conv(c)
            if tap == "conv" and c == tap_chunk:
                do_tap([("u16", u16[:, :], [128, 16 * 512], F32)])
                done = True
                break
            x_proj(c)
            if tap == "xproj" and c == tap_chunk:
                do_tap([("xssmT", xssmT[:, :], [33, 512], F32)])
                done = True
                break
            scan_ffn(c)
            if tap == "scan" and c == tap_chunk:
                do_tap([("y16", y16[:, :], [128, 16 * 512], F32)])
                done = True
                break
            if tap == "hcar" and c == tap_chunk:
                do_tap([("hcar", hcar[:, :], [128, 256], F32)])
                done = True
                break
            if tap == "ffn" and c == tap_chunk:
                do_tap([("ffn_acc", ffn_acc[:, :], [128, 8 * 512], F32)])
                done = True
                break
            yz(c)
            out_proj(c)
            if tap == "outproj" and c == tap_chunk:
                do_tap([("opT", y16[:, 0:8 * 512], [128, 8 * 512], F32)])
                done = True
                break
            merge(c)

        if not done:
            B.wait("sy", "SY", mark["out_store"])

        with nc.Block() as block:
            @block.tensor
            def _(eng):
                B.replay("pe", eng)

            @block.vector
            def _(eng):
                B.replay("dv", eng)

            @block.scalar
            def _(eng):
                B.replay("ac", eng)

            @block.sync
            def _(eng):
                B.replay("sy", eng)

            @block.gpsimd
            def _(eng):
                B.replay("gp", eng)

    return nc


def kernel(**inputs):
    x = np.ascontiguousarray(np.asarray(inputs["L0_attn_x"]))
    names = dict(
        norm_w="L0_attn_norm_w", norm_b="L0_attn_norm_b", in_proj_w="L0_attn_in_proj_w",
        conv_w="L0_attn_conv_w", conv_b="L0_attn_conv_b", x_proj_w="L0_attn_x_proj_w",
        dt_w="L0_attn_dt_w", dt_b="L0_attn_dt_b", out_proj_w="L0_attn_out_proj_w",
        gate_w="L0_ffn_gate_w", sh_gate="L0_ffn_sh_gate", sh_up="L0_ffn_sh_up",
        sh_down="L0_ffn_sh_down", rt_gate="L0_ffn_rt_gate", rt_up="L0_ffn_rt_up",
        rt_down="L0_ffn_rt_down")
    common = {k: np.ascontiguousarray(np.asarray(inputs[v])) for k, v in names.items()}
    nc = build()
    in_maps = [dict(common, x=np.ascontiguousarray(x[b])) for b in range(8)]
    res = run_bass_kernel_spmd(nc, in_maps, core_ids=list(range(8)))
    return np.stack([np.asarray(r["out"]) for r in res.results], axis=0)


# version 05
# speedup vs baseline: 1.0397x; 1.0397x over previous
import numpy as np
from contextlib import ExitStack

import concourse.bass as bass
import concourse.mybir as mybir
from concourse.bass_utils import run_bass_kernel_spmd

F32 = mybir.dt.float32
BF16 = mybir.dt.bfloat16
AF = mybir.ActivationFunctionType
ALU = mybir.AluOpType
AX = mybir.AxisListType

T, D, DI, DS = 8192, 1024, 2048, 16
L = 512
NTT = L // 128
EPS = 1e-5
NEXP = 9


class Bld:

    ENGS = ("pe", "dv", "ac", "sy", "gp")
    SEM_OF = dict(pe="PE", dv="DV", ac="AC", gp="GPC")

    def __init__(self, nc):
        self.nc = nc
        self.q = {e: [] for e in self.ENGS}
        self.cnt = dict(PE=0, DV=0, AC=0, GPC=0, SY=0, GP=0)
        self.sem = {}

    def mksems(self, ctx):
        for name in self.cnt:
            self.sem[name] = ctx.enter_context(self.nc.semaphore(name))

    def wait(self, eng, sname, val):
        if val <= 0:
            return
        self.q[eng].append(("w", sname, int(val)))

    def op(self, eng, fn):
        sname = self.SEM_OF[eng]
        self.cnt[sname] += 1
        self.q[eng].append(("i", fn, sname, 1))
        return self.cnt[sname]

    def dma(self, eng, out, in_, **kw):
        sname = {"sy": "SY", "gp": "GP"}[eng]
        self.cnt[sname] += 16
        self.q[eng].append(("d", out, in_, sname, kw))
        return self.cnt[sname]

    def replay(self, key, eng):
        for item in self.q[key]:
            k = item[0]
            if k == "w":
                eng.wait_ge(self.sem[item[1]], item[2])
            elif k == "i":
                ins = item[1](eng)
                ins.then_inc(self.sem[item[2]], item[3])
            else:
                _, out, in_, sname, kw = item
                eng.dma_start(out=out, in_=in_, **kw).then_inc(self.sem[sname], 16)


def build(n_chunks=16, tap=None, ffn_ems=None, tap_chunk=0):
    nc = bass.Bass("TRN2", target_bir_lowering=False, debug=False)
    NCH = n_chunks

    xin = nc.dram_tensor("x", [T, D], F32, kind="ExternalInput").ap()
    w_normw = nc.dram_tensor("norm_w", [D], F32, kind="ExternalInput").ap()
    w_normb = nc.dram_tensor("norm_b", [D], F32, kind="ExternalInput").ap()
    w_inproj = nc.dram_tensor("in_proj_w", [2 * DI, D], F32, kind="ExternalInput").ap()
    w_conv = nc.dram_tensor("conv_w", [DI, DI, 4], F32, kind="ExternalInput").ap()
    w_convb = nc.dram_tensor("conv_b", [DI], F32, kind="ExternalInput").ap()
    w_xproj = nc.dram_tensor("x_proj_w", [33, DI], F32, kind="ExternalInput").ap()
    w_dtw = nc.dram_tensor("dt_w", [DI, 1], F32, kind="ExternalInput").ap()
    w_dtb = nc.dram_tensor("dt_b", [DI], F32, kind="ExternalInput").ap()
    w_outproj = nc.dram_tensor("out_proj_w", [D, DI], F32, kind="ExternalInput").ap()
    w_gate = nc.dram_tensor("gate_w", [D, 8], F32, kind="ExternalInput").ap()
    w_shg = nc.dram_tensor("sh_gate", [D, DI], F32, kind="ExternalInput").ap()
    w_shu = nc.dram_tensor("sh_up", [D, DI], F32, kind="ExternalInput").ap()
    w_shd = nc.dram_tensor("sh_down", [DI, D], F32, kind="ExternalInput").ap()
    w_rtg = nc.dram_tensor("rt_gate", [8, D, DI], F32, kind="ExternalInput").ap()
    w_rtu = nc.dram_tensor("rt_up", [8, D, DI], F32, kind="ExternalInput").ap()
    w_rtd = nc.dram_tensor("rt_down", [8, DI, D], F32, kind="ExternalInput").ap()

    out = nc.dram_tensor("out", [T, D], F32, kind="ExternalOutput").ap()

    sc_wi = nc.dram_tensor("sc_wi", [32, 128, 8, 128], BF16, kind="Internal").ap()
    sc_cv = nc.dram_tensor("sc_cv", [16, 4, 128, 16, 128], BF16, kind="Internal").ap()
    sc_wo = nc.dram_tensor("sc_wo", [8, 128, 16, 128], BF16, kind="Internal").ap()
    sc_gu = nc.dram_tensor("sc_gu", [NEXP, 16, 128, 2, 8, 128], BF16, kind="Internal").ap()
    sc_dn = nc.dram_tensor("sc_dn", [NEXP, 8, 128, 16, 128], BF16, kind="Internal").ap()
    sc_ssm = nc.dram_tensor("sc_ssm", [2, 33, L], F32, kind="Internal").ap()
    sc_mrow = nc.dram_tensor("sc_mrow", [2, 8, L], F32, kind="Internal").ap()

    with ExitStack() as ctx:
        sb = lambda name, shape, dt: ctx.enter_context(nc.sbuf_tensor(name, shape, dt))
        ps = lambda name, shape, dt: ctx.enter_context(nc.psum_tensor(name, shape, dt))

        ident_f = sb("ident_f", [128, 128], F32)
        ident_b = sb("ident_b", [128, 128], BF16)
        ones128 = sb("ones128", [128, 128], F32)
        negdtw = sb("negdtw", [128, 16], F32)
        negdtb = sb("negdtb", [128, 16], F32)
        convb = sb("convb", [128, 16], F32)
        normw_fm = sb("normw_fm", [128, 8], F32)
        normb_fm = sb("normb_fm", [128, 8], F32)
        gatew_f32 = sb("gatew_f32", [128, 8 * 8], F32)
        xprojT_sb = sb("xprojT_sb", [128, 16 * 33], BF16)

        u16 = sb("u16", [128, 16 * 512], F32)
        y16 = sb("y16", [128, 16 * 512], F32)
        arnI = sb("arnI", [128, 4224], F32)
        arnJ = sb("arnJ", [128, 1024], F32)
        arnE = sb("arnE", [128, 8 * 512], BF16)
        xnT_bf = sb("xnT_bf", [128, 8 * 512], BF16)
        siluz_bf = sb("siluz_bf", [128, 16 * 512], BF16)
        arnD = sb("arnD", [128, 16 * 512], BF16)
        ffn_acc = sb("ffn_acc", [128, 8 * 512], F32)
        hist_bf = sb("hist_bf", [128, 16 * 4], BF16)
        hcar = sb("hcar", [128, 256], F32)
        dsrep = sb("dsrep", [128, 512], F32)
        brep = sb("brep", [128, 2 * 512], F32)
        crep = sb("crep", [128, 2 * 512], F32)
        bu2 = sb("bu2", [128, 2 * 512], F32)
        h2 = sb("h2", [128, 2 * 512], F32)
        a2 = sb("a2", [128, 4 * 512], F32)
        mrep = sb("mrep", [128, 2 * 512], BF16)
        xssmT = sb("xssmT", [33, 512], F32)
        lgT = xssmT[0:8, :]
        lgtok = sb("lgtok", [128, 32], F32)
        topv = sb("topv", [128, 4], F32)
        scoret = sb("scoret", [128, 4], F32)
        mtok = sb("mtok", [128, 32], F32)
        mrowsT = sb("mrowsT", [8, 512], F32)
        stats = sb("stats", [128, 16], F32)
        epsv = sb("epsv", [128, 1], F32)
        wa = sb("wa", [128, 8192], BF16)

        b0 = ps("b0", [128, 512], F32)
        b1 = ps("b1", [128, 512], F32)
        b2 = ps("b2", [128, 512], F32)
        b3 = ps("b3", [128, 512], F32)
        b4 = ps("b4", [128, 512], F32)
        b5 = ps("b5", [128, 512], F32)
        bt = ps("bt", [128, 1024], BF16)
        bx = ps("bx", [128, 512], F32)

        B = Bld(nc)
        B.mksems(ctx)

        arnI_bf = arnI[:, :].bitcast(BF16)
        arnJ_bf = arnJ[:, :].bitcast(BF16)

        def xin_slice(i):
            return arnI_bf[:, i * 520:(i + 1) * 520]

        def yz_slice(i):
            return arnI_bf[:, i * 520:i * 520 + 512]

        mark = {}

        B.op("gp", lambda e: e.memset(ones128[:, :], 1.0))
        B.op("gp", lambda e: e.affine_select(
            out=ident_f[:, :], in_=ones128[:, :], pattern=[[-1, 128]],
            compare_op=ALU.is_equal, fill=0.0, base=0, channel_multiplier=1))
        gpc_idf = B.cnt["GPC"]
        B.wait("dv", "GPC", gpc_idf)
        B.op("dv", lambda e: e.tensor_copy(out=ident_b[:, :], in_=ident_f[:, :]))
        B.op("dv", lambda e: e.memset(hcar[:, :], 0.0))
        B.op("dv", lambda e: e.memset(hist_bf[:, :], 0.0))
        B.op("dv", lambda e: e.memset(epsv[:, :], EPS))
        dv_idb = B.cnt["DV"]

        B.dma("gp", gatew_f32[:, :].rearrange("p (kt e) -> p kt e", kt=8),
              w_gate.rearrange("(kt p) e -> p kt e", p=128),
              allow_slow_non_contiguous=True)

        tmpv = arnJ[0:33, 0:128]
        last_vec_dv = 0
        last_vec_pe = 0

        def vec_tile(dst, src_ap, n_kt, scale):
            nonlocal last_vec_dv, last_vec_pe
            B.wait("gp", "PE", last_vec_pe)
            ld = B.dma("gp", tmpv[0:n_kt, :], src_ap)
            B.wait("pe", "GP", ld)
            B.wait("pe", "DV", max(dv_idb, last_vec_dv))
            B.wait("pe", "GPC", gpc_idf)
            pe_c = B.op("pe", lambda e, n_kt=n_kt: e.transpose(
                out=bx[:, 0:n_kt], in_=tmpv[0:n_kt, :],
                identity=ident_f[0:n_kt, 0:n_kt]))
            last_vec_pe = pe_c
            B.wait("dv", "PE", pe_c)
            B.op("dv", lambda e, dst=dst, n_kt=n_kt, scale=scale: e.tensor_scalar(
                out=dst[:, 0:n_kt], in0=bx[:, 0:n_kt],
                scalar1=float(scale), scalar2=None, op0=ALU.mult))
            last_vec_dv = B.cnt["DV"]
            return last_vec_dv

        vec_tile(negdtw, w_dtw.rearrange("(a b) one -> a (b one)", b=128), 16, -1.0)
        vec_tile(negdtb, w_dtb.rearrange("(a b) -> a b", b=128), 16, -1.0)
        vec_tile(convb, w_convb.rearrange("(a b) -> a b", b=128), 16, 1.0)
        vec_tile(normw_fm, w_normw.rearrange("(a b) -> a b", b=128), 8, 1.0)
        vec_tile(normb_fm, w_normb.rearrange("(a b) -> a b", b=128), 8, 1.0)

        xp_nat = arnD[0:33, 0:2048]
        ldxp = B.dma("gp", xp_nat, w_xproj)
        B.wait("pe", "GP", ldxp)
        for kt in range(16):
            bank = kt % 2
            pe_c = B.op("pe", lambda e, kt=kt, bank=bank: e.transpose(
                out=bt[:, bank * 512:bank * 512 + 33],
                in_=xp_nat[:, kt * 128:(kt + 1) * 128],
                identity=ident_b[0:33, 0:33]))
            B.wait("dv", "PE", pe_c)
            B.op("dv", lambda e, kt=kt, bank=bank: e.tensor_copy(
                out=xprojT_sb[:, kt * 33:(kt + 1) * 33],
                in_=bt[:, bank * 512:bank * 512 + 33]))
            B.wait("pe", "DV", B.cnt["DV"] - 1)
        dv_xp = B.cnt["DV"]

        slotA = wa[:, 0:8192]
        prepB = arnD[:, 4096:4608]
        stage_cv = arnD[:, 6144:8192]

        last_store = 0

        def prep_transposed():
            nonlocal last_store
            natv = slotA[:, 0:4096].rearrange("p (jt d) -> p jt d", jt=4)
            for jg in range(8):
                B.wait("gp", "SY", last_store)
                B.wait("gp", "PE", mark.get("prep_pe", 0))
                ld = B.dma("gp", natv,
                           w_inproj[jg * 512:(jg + 1) * 512, :].rearrange(
                               "(jt p) d -> p jt d", p=128))
                B.wait("pe", "GP", ld)
                for dkt in range(8):
                    bank_half = dkt % 2
                    bo = bank_half * 512
                    B.wait("pe", "DV", mark.get("prep_dv", 0))
                    for jt in range(4):
                        B.op("pe", lambda e, jt=jt, dkt=dkt, bo=bo: e.transpose(
                            out=bt[:, bo + jt * 128:bo + (jt + 1) * 128],
                            in_=natv[:, jt, dkt * 128:(dkt + 1) * 128],
                            identity=ident_b[:, :]))
                    pe_c = B.cnt["PE"]
                    mark["prep_pe"] = pe_c
                    B.wait("dv", "PE", pe_c)
                    B.wait("dv", "SY", last_store)
                    dv_c = B.op("dv", lambda e, bo=bo: e.tensor_copy(
                        out=prepB, in_=bt[:, bo:bo + 512]))
                    mark["prep_dv"] = dv_c
                    B.wait("sy", "DV", dv_c)
                    mb = jg * 4
                    last_store = B.dma(
                        "sy", sc_wi[mb:mb + 4, :, dkt, :].rearrange("m p f -> p m f"),
                        prepB.rearrange("p (m f) -> p m f", m=4))
            natc = slotA[:, 0:8192]
            natc_v = natc.rearrange("p (i k) -> p k i", k=4)
            for mt in range(16):
                B.wait("gp", "SY", last_store)
                B.wait("gp", "PE", mark.get("prep_pe", 0))
                ld = B.dma("gp", natc,
                           w_conv[mt * 128:(mt + 1) * 128, :, :].rearrange("p i k -> p (i k)"))
                B.wait("pe", "GP", ld)
                for k in range(4):
                    for ikg in range(4):
                        bo = (ikg % 2) * 512
                        B.wait("pe", "DV", mark.get("prep_dv", 0))
                        for q in range(4):
                            ikt = ikg * 4 + q
                            B.op("pe", lambda e, k=k, ikt=ikt, q=q, bo=bo: e.transpose(
                                out=bt[:, bo + q * 128:bo + (q + 1) * 128],
                                in_=natc_v[:, k, ikt * 128:(ikt + 1) * 128],
                                identity=ident_b[:, :]))
                        pe_c = B.cnt["PE"]
                        mark["prep_pe"] = pe_c
                        B.wait("dv", "PE", pe_c)
                        B.wait("dv", "SY", last_store)
                        dv_c = B.op("dv", lambda e, ikg=ikg, bo=bo: e.tensor_copy(
                            out=stage_cv[:, ikg * 512:(ikg + 1) * 512],
                            in_=bt[:, bo:bo + 512]))
                        mark["prep_dv"] = dv_c
                    dv_c = B.cnt["DV"]
                    B.wait("sy", "DV", dv_c)
                    last_store = B.dma(
                        "sy", sc_cv[mt, k, :, :, :].rearrange("p i f -> p (i f)"),
                        stage_cv)
            nato = slotA[:, 0:8192].rearrange("p (dt j) -> p dt j", dt=4)
            for dg in range(2):
                B.wait("gp", "SY", last_store)
                B.wait("gp", "PE", mark.get("prep_pe", 0))
                ld = B.dma("gp", nato,
                           w_outproj[dg * 512:(dg + 1) * 512, :].rearrange(
                               "(dt p) j -> p dt j", p=128))
                B.wait("pe", "GP", ld)
                for jkt in range(16):
                    bo = (jkt % 2) * 512
                    B.wait("pe", "DV", mark.get("prep_dv", 0))
                    for dt in range(4):
                        B.op("pe", lambda e, dt=dt, jkt=jkt, bo=bo: e.transpose(
                            out=bt[:, bo + dt * 128:bo + (dt + 1) * 128],
                            in_=nato[:, dt, jkt * 128:(jkt + 1) * 128],
                            identity=ident_b[:, :]))
                    pe_c = B.cnt["PE"]
                    mark["prep_pe"] = pe_c
                    B.wait("dv", "PE", pe_c)
                    B.wait("dv", "SY", last_store)
                    dv_c = B.op("dv", lambda e, bo=bo: e.tensor_copy(
                        out=prepB, in_=bt[:, bo:bo + 512]))
                    mark["prep_dv"] = dv_c
                    B.wait("sy", "DV", dv_c)
                    last_store = B.dma(
                        "sy", sc_wo[dg * 4:(dg + 1) * 4, :, jkt, :].rearrange("m p f -> p m f"),
                        prepB.rearrange("p (m f) -> p m f", m=4))

        def prep_moe():
            nonlocal last_store
            srcs_gu = [(w_rtg[e2], w_rtu[e2]) for e2 in range(8)] + [(w_shg, w_shu)]
            srcs_dn = [w_rtd[e2] for e2 in range(8)] + [w_shd]
            stgs = [slotA[:, 0:2048], slotA[:, 2048:4096]]
            last_use = [last_store, last_store]
            par = 0
            for em in range(NEXP):
                g_src, u_src = srcs_gu[em]
                for gu in range(2):
                    src = (g_src, u_src)[gu]
                    for kt in range(8):
                        stg = stgs[par]
                        B.wait("gp", "SY", last_use[par])
                        ld = B.dma("gp", stg, src[kt * 128:(kt + 1) * 128, :])
                        B.wait("sy", "GP", ld)
                        last_use[par] = B.dma(
                            "sy", sc_gu[em, :, :, gu, kt, :].rearrange("m p f -> p m f"),
                            stg.rearrange("p (m f) -> p m f", m=16))
                        par ^= 1
                for kt in range(16):
                    stg = stgs[par][:, 0:1024]
                    B.wait("gp", "SY", last_use[par])
                    ld = B.dma("gp", stg, srcs_dn[em][kt * 128:(kt + 1) * 128, :])
                    B.wait("sy", "GP", ld)
                    last_use[par] = B.dma(
                        "sy", sc_dn[em, :, :, kt, :].rearrange("m p f -> p m f"),
                        stg.rearrange("p (m f) -> p m f", m=8))
                    par ^= 1
            last_store = max(last_use)

        prep_transposed()
        prep_moe()

        prep_sy, prep_gp, prep_dv = B.cnt["SY"], B.cnt["GP"], B.cnt["DV"]
        for eng in ("pe", "dv", "ac", "sy", "gp"):
            B.wait(eng, "SY", prep_sy)
            B.wait(eng, "GP", prep_gp)
            B.wait(eng, "DV", prep_dv)

        bank_mark = {}

        def bank_wait(eng, bank):
            s, c = bank_mark.get(id(bank), ("DV", 0))
            B.wait(eng, s, c)

        def bank_consumed(bank, sem, cnt):
            bank_mark[id(bank)] = (sem, cnt)

        wslots = [wa[:, i * 2048:(i + 1) * 2048] for i in range(4)]
        wslot_mark = [0] * 4
        wslot_load = [0] * 4
        wrr = [0]

        def wload(src_ap, ncols):
            i = wrr[0] % 4
            wrr[0] += 1
            B.wait("sy", "PE", wslot_mark[i])
            wslot_load[i] = B.dma("sy", wslots[i][:, 0:ncols], src_ap)
            return i

        def wuse(i):
            B.wait("pe", "SY", wslot_load[i])

        def wdone(i):
            wslot_mark[i] = B.cnt["PE"]

        def tok_ln(dst, xslot, xmm, sq, mu_col, var_col, rstd_col):
            B.op("ac", lambda e: e.activation(out=dst, in_=xslot, func=AF.Copy,
                                              accum_out=stats[:, mu_col:mu_col + 1]))
            acc = B.cnt["AC"]
            B.wait("ac", "AC", acc)
            B.op("ac", lambda e: e.activation(out=stats[:, mu_col + 1:mu_col + 2],
                                              in_=stats[:, mu_col:mu_col + 1],
                                              func=AF.Copy, scale=-1.0 / D))
            acc = B.cnt["AC"]
            B.wait("ac", "AC", acc)
            B.op("ac", lambda e: e.activation(out=xmm, in_=xslot, func=AF.Identity,
                                              bias=stats[:, mu_col + 1:mu_col + 2]))
            B.op("ac", lambda e: e.activation(out=sq, in_=xmm, func=AF.Square,
                                              accum_out=stats[:, var_col:var_col + 1]))
            acc = B.cnt["AC"]
            B.wait("ac", "AC", acc)
            B.op("ac", lambda e: e.activation(out=stats[:, var_col + 1:var_col + 2],
                                              in_=stats[:, var_col:var_col + 1],
                                              func=AF.Sqrt, bias=epsv[:, 0:1],
                                              scale=1.0 / D))
            acc = B.cnt["AC"]
            B.wait("dv", "AC", acc)
            dvr = B.op("dv", lambda e: e.reciprocal(out=stats[:, rstd_col:rstd_col + 1],
                                                    in_=stats[:, var_col + 1:var_col + 2]))
            B.wait("ac", "DV", dvr)
            B.op("ac", lambda e: e.activation(out=dst, in_=xmm, func=AF.Identity,
                                              scale=stats[:, rstd_col:rstd_col + 1]))
            return B.cnt["AC"]

        def ln_zone(c):
            xslot = arnI[:, 0:1024]
            xmm = arnI[:, 1024:2048]
            xn_t = arnI[:, 2048:3072]
            raw2 = arnI[:, 3072:4096]
            sq = arnJ[:, 0:1024]
            xnbf_st = arnJ_bf[:, 0:1024]
            r2bf_st = arnJ_bf[:, 1024:2048]
            xfree = 0
            for tt in range(NTT):
                B.wait("gp", "SY", mark.get("out_store", 0))
                B.wait("gp", "DV", mark.get("merge_x_used", 0))
                B.wait("gp", "AC", xfree)
                ld = B.dma("gp", xslot,
                           xin[c * L + tt * 128: c * L + (tt + 1) * 128, :])
                B.wait("ac", "GP", ld)
                B.wait("ac", "PE", mark.get("pe_gate", 0))
                B.wait("ac", "SY", mark.get("out_store", 0))
                tok_ln(xn_t, xslot, xmm, sq, 0, 2, 4)
                tok_ln(raw2, xn_t, xmm, sq, 5, 7, 9)
                B.op("ac", lambda e: e.activation(out=xnbf_st, in_=xn_t, func=AF.Copy))
                dv_xnbf = B.cnt["AC"]
                B.op("ac", lambda e: e.activation(out=r2bf_st, in_=raw2, func=AF.Copy))
                dv_r2bf = B.cnt["AC"]
                mark["lg_tmpf_free"] = B.cnt["DV"]
                for src_st, ac_src, affine in ((xnbf_st, dv_xnbf, False),
                                               (r2bf_st, dv_r2bf, True)):
                    srcv = src_st.rearrange("p (dt f) -> p dt f", dt=8)
                    for half in range(2):
                        bank_wait("pe", bt)
                        B.wait("pe", "AC", ac_src)
                        for g in range(4):
                            dt = half * 4 + g
                            B.op("pe", lambda e, dt=dt, g=g, srcv=srcv: e.transpose(
                                out=bt[:, g * 128:(g + 1) * 128],
                                in_=srcv[:, dt, :], identity=ident_b[:, :]))
                        pe_c = B.cnt["PE"]
                        B.wait("ac", "PE", pe_c)
                        for g in range(4):
                            dt = half * 4 + g
                            col = dt * 512 + tt * 128
                            if affine:
                                B.op("ac", lambda e, dt=dt, g=g, col=col: e.activation(
                                    out=arnE[:, col:col + 128],
                                    in_=bt[:, g * 128:(g + 1) * 128], func=AF.Identity,
                                    scale=normw_fm[:, dt:dt + 1],
                                    bias=normb_fm[:, dt:dt + 1]))
                            else:
                                B.op("ac", lambda e, g=g, col=col: e.activation(
                                    out=xnT_bf[:, col:col + 128],
                                    in_=bt[:, g * 128:(g + 1) * 128], func=AF.Copy))
                        bank_consumed(bt, "AC", B.cnt["AC"])
                tmpf = xmm
                ac_c = 0
                for half in range(2):
                    bank_wait("pe", bx)
                    for g in range(4):
                        dt = half * 4 + g
                        B.op("pe", lambda e, dt=dt, g=g: e.transpose(
                            out=bx[:, g * 128:(g + 1) * 128],
                            in_=xn_t[:, dt * 128:(dt + 1) * 128],
                            identity=ident_f[:, :]))
                    pe_c = B.cnt["PE"]
                    B.wait("ac", "PE", pe_c)
                    B.wait("ac", "DV", mark.get("lg_tmpf_free", 0))
                    ac_c = B.op("ac", lambda e, half=half: e.activation(
                        out=tmpf[:, half * 512:(half + 1) * 512], in_=bx[:, 0:512],
                        func=AF.Copy))
                    bank_consumed(bx, "AC", ac_c)
                B.wait("pe", "AC", ac_c)
                bank_wait("pe", b2)
                for dt in range(8):
                    B.op("pe", lambda e, dt=dt, tt=tt: e.matmul(
                        out=b2[0:8, tt * 128:(tt + 1) * 128],
                        lhsT=gatew_f32[:, dt * 8:(dt + 1) * 8],
                        rhs=tmpf[:, dt * 128:(dt + 1) * 128],
                        start=(dt == 0), stop=(dt == 7)))
                pe_gate = B.cnt["PE"]
                mark["pe_gate"] = pe_gate
                B.wait("ac", "PE", pe_gate)
                B.op("ac", lambda e, tt=tt: e.activation(
                    out=lgT[:, tt * 128:(tt + 1) * 128],
                    in_=b2[0:8, tt * 128:(tt + 1) * 128], func=AF.Copy))
                bank_consumed(b2, "AC", B.cnt["AC"])
                xfree = B.cnt["AC"]
            mark["dv_lnT"] = B.cnt["DV"]
            mark["ac_lnT"] = B.cnt["AC"]

        def mask_zone(c):
            bank_wait("pe", bx)
            B.wait("pe", "AC", mark["ac_lnT"])
            for tt in range(NTT):
                B.op("pe", lambda e, tt=tt: e.transpose(
                    out=bx[:, tt * 8:(tt + 1) * 8],
                    in_=lgT[:, tt * 128:(tt + 1) * 128],
                    identity=ident_f[0:8, 0:8]))
            pe_tp = B.cnt["PE"]
            B.wait("ac", "PE", pe_tp)
            ac_lgtok = B.op("ac", lambda e: e.activation(out=lgtok[:, :], in_=bx[:, 0:32],
                                                         func=AF.Copy))
            bank_consumed(bx, "AC", ac_lgtok)
            B.wait("dv", "AC", ac_lgtok)
            for tt in range(NTT):
                B.op("dv", lambda e, tt=tt: e.reduce_max(
                    out=topv[:, tt:tt + 1], in_=lgtok[:, tt * 8:(tt + 1) * 8], axis=AX.X))
            dv_top = B.cnt["DV"]
            B.wait("ac", "DV", dv_top)
            B.op("ac", lambda e: e.activation(out=scoret[:, 0:4], in_=topv[:, 0:4],
                                              func=AF.Sigmoid))
            ac_sc = B.cnt["AC"]
            B.wait("dv", "AC", ac_sc)
            for tt in range(NTT):
                B.op("dv", lambda e, tt=tt: e.tensor_scalar(
                    out=mtok[:, tt * 8:(tt + 1) * 8], in0=lgtok[:, tt * 8:(tt + 1) * 8],
                    scalar1=topv[:, tt:tt + 1], scalar2=scoret[:, tt:tt + 1],
                    op0=ALU.is_equal, op1=ALU.mult))
            dv_mtok = B.cnt["DV"]
            bank_wait("pe", bx)
            B.wait("pe", "DV", dv_mtok)
            for tt in range(NTT):
                B.op("pe", lambda e, tt=tt: e.transpose(
                    out=bx[0:8, tt * 128:(tt + 1) * 128], in_=mtok[:, tt * 8:(tt + 1) * 8],
                    identity=ident_f[:, :]))
            pe_mr = B.cnt["PE"]
            B.wait("ac", "PE", pe_mr)
            dv_mr = B.op("ac", lambda e: e.activation(out=mrowsT[:, :], in_=bx[0:8, 0:512],
                                                      func=AF.Copy))
            bank_consumed(bx, "AC", dv_mr)
            B.wait("gp", "AC", dv_mr)
            B.wait("gp", "GP", mark.get(("mrep_used", c % 2), 0))
            mark[("mrow_store", c % 2)] = B.dma("gp", sc_mrow[c % 2], mrowsT[:, :])

        def in_proj(c):
            B.wait("dv", "SY", mark.get("out_store", 0))
            for i in range(16):
                B.op("dv", lambda e, i=i: e.tensor_copy(
                    out=xin_slice(i)[:, 0:3], in_=hist_bf[:, i * 4:i * 4 + 3]))
            dv_hist = B.cnt["DV"]
            banks = (b0, b1)
            for m in range(32):
                si = wload(sc_wi[m].rearrange("p kt f -> p (kt f)"), 1024)
                bk = banks[m % 2]
                wuse(si)
                bank_wait("pe", bk)
                B.wait("pe", "AC", mark["ac_lnT"])
                for kt in range(8):
                    B.op("pe", lambda e, kt=kt, si=si, bk=bk: e.matmul(
                        out=bk[:, :], lhsT=wslots[si][:, kt * 128:(kt + 1) * 128],
                        rhs=arnE[:, kt * 512:(kt + 1) * 512],
                        start=(kt == 0), stop=(kt == 7)))
                wdone(si)
                pe_c = B.cnt["PE"]
                if m < 16:
                    B.wait("ac", "PE", pe_c)
                    B.wait("ac", "DV", dv_hist)
                    B.op("ac", lambda e, m=m, bk=bk: e.activation(
                        out=xin_slice(m)[:, 3:515], in_=bk[:, :], func=AF.Copy))
                    bank_consumed(bk, "AC", B.cnt["AC"])
                else:
                    B.wait("ac", "PE", pe_c)
                    B.wait("ac", "DV", mark.get("dv_yz", 0))
                    B.op("ac", lambda e, m=m, bk=bk: e.activation(
                        out=siluz_bf[:, (m - 16) * 512:(m - 15) * 512],
                        in_=bk[:, :], func=AF.Silu))
                    bank_consumed(bk, "AC", B.cnt["AC"])
            mark["dv_xin"] = B.cnt["DV"]
            mark["ac_xin"] = B.cnt["AC"]

        def conv(c):
            banks = (b0, b1)
            for m in range(16):
                slots = [wload(sc_cv[m, k].rearrange("p i f -> p (i f)"), 2048)
                         for k in range(4)]
                bk = banks[m % 2]
                bank_wait("pe", bk)
                B.wait("pe", "AC", mark["ac_xin"])
                first = True
                for k in range(4):
                    si = slots[k]
                    wuse(si)
                    for ikt in range(16):
                        B.op("pe", lambda e, k=k, ikt=ikt, si=si, bk=bk, first=first: e.matmul(
                            out=bk[:, :], lhsT=wslots[si][:, ikt * 128:(ikt + 1) * 128],
                            rhs=xin_slice(ikt)[:, k:k + 512],
                            start=first, stop=(k == 3 and ikt == 15)))
                        first = False
                    wdone(si)
                pe_c = B.cnt["PE"]
                B.wait("ac", "PE", pe_c)
                B.wait("ac", "DV", mark.get("dv_scan_u_free", 0))
                B.op("ac", lambda e, m=m, bk=bk: e.activation(
                    out=u16[:, m * 512:(m + 1) * 512], in_=bk[:, :], func=AF.Silu,
                    bias=convb[:, m:m + 1]))
                ac_c = B.cnt["AC"]
                bank_consumed(bk, "AC", ac_c)
                B.op("ac", lambda e, m=m: e.activation(
                    out=arnD[:, m * 512:(m + 1) * 512],
                    in_=u16[:, m * 512:(m + 1) * 512], func=AF.Copy))
            ac_all = B.cnt["AC"]
            B.wait("dv", "AC", ac_all)
            for i in range(16):
                B.op("dv", lambda e, i=i: e.tensor_copy(
                    out=hist_bf[:, i * 4:i * 4 + 3], in_=xin_slice(i)[:, 512:515]))
            mark["ac_xact"] = ac_all
            mark["ac_u"] = ac_all

        def x_proj(c):
            bank_wait("pe", b2)
            B.wait("pe", "AC", mark["ac_xact"])
            for kt in range(16):
                B.op("pe", lambda e, kt=kt: e.matmul(
                    out=b2[0:33, :], lhsT=xprojT_sb[:, kt * 33:(kt + 1) * 33],
                    rhs=arnD[:, kt * 512:(kt + 1) * 512],
                    start=(kt == 0), stop=(kt == 15)))
            pe_c = B.cnt["PE"]
            B.wait("ac", "PE", pe_c)
            dv_c = B.op("ac", lambda e: e.activation(out=xssmT[:, :], in_=b2[0:33, :],
                                                     func=AF.Copy))
            bank_consumed(b2, "AC", dv_c)
            B.wait("gp", "AC", dv_c)
            B.wait("gp", "GP", mark.get(("ssm_used", c % 2), 0))
            st = B.dma("gp", sc_ssm[c % 2], xssmT[:, :])
            B.wait("gp", "GP", st)
            B.wait("gp", "AC", mark.get("ac_scan", 0))
            B.dma("gp", dsrep[:, :], sc_ssm[c % 2, 0:1, :].partition_broadcast(128))
            B.dma("gp", brep[:, 0:512], sc_ssm[c % 2, 1:2, :].partition_broadcast(128))
            mark["gp_rep0"] = B.dma(
                "gp", crep[:, 0:512], sc_ssm[c % 2, 17:18, :].partition_broadcast(128))

        def scan_ffn(c):
            prev = mark.get("dv_y", 0)
            dv_used_par = [prev, prev]
            a_used_par = [prev, prev, prev, prev]
            prod_used = [mark.get("gpc_y", 0), mark.get("gpc_y", 0)]
            rep_ready = {0: mark["gp_rep0"]}
            B.wait("dv", "PE", mark.get("pe_merge", 0))
            B.wait("gp", "PE", mark.get("pe_merge", 0))
            prevf = mark.get("dv_ffn", 0)
            mrep_used = [prevf, prevf]
            xmT_pe_used = [0]
            emitted = [0]

            def emit_expert():
                em = emitted[0]
                emitted[0] += 1
                epar = em % 2
                if em < 8:
                    B.wait("gp", "GP", mark[("mrow_store", c % 2)])
                    B.wait("gp", "DV", mrep_used[epar])
                    ld = B.dma("gp", mrep[:, epar * 512:(epar + 1) * 512],
                               sc_mrow[c % 2, em:em + 1, :].partition_broadcast(128))
                    B.wait("dv", "GP", ld)
                    B.wait("dv", "PE", xmT_pe_used[0])
                    for dt in range(8):
                        B.op("dv", lambda e, dt=dt, epar=epar: e.tensor_tensor(
                            out=arnE[:, dt * 512:(dt + 1) * 512],
                            in0=xnT_bf[:, dt * 512:(dt + 1) * 512],
                            in1=mrep[:, epar * 512:(epar + 1) * 512], op=ALU.mult))
                    mrep_used[epar] = B.cnt["DV"]
                    dv_xmT = B.cnt["DV"]
                    rhs_buf = arnE
                else:
                    dv_xmT = 0
                    rhs_buf = xnT_bf
                for m in range(16):
                    si = wload(sc_gu[em, m].rearrange("p gu kt f -> p (gu kt f)"), 2048)
                    wuse(si)
                    bg, bu_ = (b0, b1) if m % 2 == 0 else (b2, b3)
                    bank_wait("pe", bg)
                    bank_wait("pe", bu_)
                    B.wait("pe", "DV", dv_xmT)
                    for kt in range(8):
                        B.op("pe", lambda e, si=si, kt=kt, rb=rhs_buf, bg=bg: e.matmul(
                            out=bg[:, :], lhsT=wslots[si][:, kt * 128:(kt + 1) * 128],
                            rhs=rb[:, kt * 512:(kt + 1) * 512],
                            start=(kt == 0), stop=(kt == 7)))
                    for kt in range(8):
                        B.op("pe", lambda e, si=si, kt=kt, rb=rhs_buf, bu_=bu_: e.matmul(
                            out=bu_[:, :],
                            lhsT=wslots[si][:, 1024 + kt * 128:1024 + (kt + 1) * 128],
                            rhs=rb[:, kt * 512:(kt + 1) * 512],
                            start=(kt == 0), stop=(kt == 7)))
                    wdone(si)
                    pe_c = B.cnt["PE"]
                    jpar = m % 4
                    B.wait("ac", "PE", pe_c)
                    B.wait("ac", "GPC", mark.get(("silug_used", jpar), 0))
                    B.op("ac", lambda e, jpar=jpar, bg=bg: e.activation(
                        out=arnJ_bf[:, jpar * 512:(jpar + 1) * 512], in_=bg[:, :],
                        func=AF.Silu))
                    bank_consumed(bg, "AC", B.cnt["AC"])
                    B.op("ac", lambda e, m=m, bu_=bu_: e.activation(
                        out=arnD[:, m * 512:(m + 1) * 512], in_=bu_[:, :],
                        func=AF.Copy))
                    ac_c = B.cnt["AC"]
                    bank_consumed(bu_, "AC", ac_c)
                    B.wait("gp", "AC", ac_c)
                    gp_c = B.op("gp", lambda e, m=m, jpar=jpar: e.tensor_tensor(
                        out=arnD[:, m * 512:(m + 1) * 512],
                        in0=arnJ_bf[:, jpar * 512:(jpar + 1) * 512],
                        in1=arnD[:, m * 512:(m + 1) * 512], op=ALU.mult))
                    mark[("silug_used", jpar)] = gp_c
                xmT_pe_used[0] = B.cnt["PE"]
                dv_act = B.cnt["GPC"]
                banks = (b4, b5)
                for m2 in range(8):
                    si = wload(sc_dn[em, m2].rearrange("p kt f -> p (kt f)"), 2048)
                    bk = banks[m2 % 2]
                    wuse(si)
                    bank_wait("pe", bk)
                    B.wait("pe", "GPC", dv_act)
                    for kt in range(16):
                        B.op("pe", lambda e, si=si, kt=kt, bk=bk: e.matmul(
                            out=bk[:, :], lhsT=wslots[si][:, kt * 128:(kt + 1) * 128],
                            rhs=arnD[:, kt * 512:(kt + 1) * 512],
                            start=(kt == 0), stop=(kt == 15)))
                    wdone(si)
                    pe_c = B.cnt["PE"]
                    B.wait("dv", "PE", pe_c)
                    if em == 0:
                        B.op("dv", lambda e, m2=m2, bk=bk: e.tensor_copy(
                            out=ffn_acc[:, m2 * 512:(m2 + 1) * 512], in_=bk[:, :]))
                    else:
                        B.op("dv", lambda e, m2=m2, bk=bk: e.tensor_tensor(
                            out=ffn_acc[:, m2 * 512:(m2 + 1) * 512],
                            in0=ffn_acc[:, m2 * 512:(m2 + 1) * 512], in1=bk[:, :],
                            op=ALU.add))
                    bank_consumed(bk, "DV", B.cnt["DV"])

            for s in range(DS):
                par = s % 2
                if s + 1 < DS:
                    B.wait("gp", "DV", dv_used_par[1 - par])
                    B.dma("gp", brep[:, (1 - par) * 512:(2 - par) * 512],
                          sc_ssm[c % 2, 2 + s:3 + s, :].partition_broadcast(128))
                    rep_ready[s + 1] = B.dma(
                        "gp", crep[:, (1 - par) * 512:(2 - par) * 512],
                        sc_ssm[c % 2, 18 + s:19 + s, :].partition_broadcast(128))
                B.wait("dv", "GP", rep_ready[s])
                for dt in range(16):
                    apar = dt % 4
                    B.wait("ac", "DV", a_used_par[apar])
                    if s == 0 and dt == 0:
                        B.wait("ac", "GP", mark["gp_rep0"])
                    B.op("ac", lambda e, dt=dt, apar=apar: e.activation(
                        out=a2[:, apar * 512:(apar + 1) * 512], in_=dsrep[:, :],
                        func=AF.Sigmoid, scale=negdtw[:, dt:dt + 1],
                        bias=negdtb[:, dt:dt + 1]))
                    ac_a = B.cnt["AC"]
                    bpar = dt % 2
                    B.wait("dv", "AC", mark["ac_u"])
                    B.op("dv", lambda e, dt=dt, par=par, bpar=bpar: e.tensor_tensor(
                        out=bu2[:, bpar * 512:(bpar + 1) * 512],
                        in0=u16[:, dt * 512:(dt + 1) * 512],
                        in1=brep[:, par * 512:(par + 1) * 512], op=ALU.mult))
                    B.wait("dv", "AC", ac_a)
                    idx = s * 16 + dt
                    hpar = dt % 2
                    B.op("dv", lambda e, apar=apar, bpar=bpar, hpar=hpar, idx=idx:
                         e.tensor_tensor_scan(
                             out=h2[:, hpar * 512:(hpar + 1) * 512],
                             data0=a2[:, apar * 512:(apar + 1) * 512],
                             data1=bu2[:, bpar * 512:(bpar + 1) * 512],
                             initial=hcar[:, idx:idx + 1], op0=ALU.mult, op1=ALU.add))
                    a_used_par[apar] = B.cnt["DV"]
                    if s == 0:
                        B.op("dv", lambda e, dt=dt, hpar=hpar, par=par: e.tensor_tensor(
                            out=y16[:, dt * 512:(dt + 1) * 512],
                            in0=h2[:, hpar * 512:(hpar + 1) * 512],
                            in1=crep[:, par * 512:(par + 1) * 512], op=ALU.mult))
                        B.op("dv", lambda e, hpar=hpar, idx=idx: e.tensor_copy(
                            out=hcar[:, idx:idx + 1],
                            in_=h2[:, hpar * 512 + 511: hpar * 512 + 512]))
                    else:
                        dv_p = B.op("dv", lambda e, hpar=hpar, par=par, bpar=bpar: e.tensor_tensor(
                            out=bu2[:, bpar * 512:(bpar + 1) * 512],
                            in0=h2[:, hpar * 512:(hpar + 1) * 512],
                            in1=crep[:, par * 512:(par + 1) * 512], op=ALU.mult))
                        B.op("dv", lambda e, hpar=hpar, idx=idx: e.tensor_copy(
                            out=hcar[:, idx:idx + 1],
                            in_=h2[:, hpar * 512 + 511: hpar * 512 + 512]))
                        B.op("dv", lambda e, dt=dt, bpar=bpar: e.tensor_tensor(
                            out=y16[:, dt * 512:(dt + 1) * 512],
                            in0=y16[:, dt * 512:(dt + 1) * 512],
                            in1=bu2[:, bpar * 512:(bpar + 1) * 512], op=ALU.add))
                dv_used_par[par] = B.cnt["DV"]
                want = min(NEXP, ((s + 1) * NEXP + DS - 1) // DS)
                while emitted[0] < want:
                    emit_expert()
            while emitted[0] < NEXP:
                emit_expert()
            mark[("ssm_used", c % 2)] = B.cnt["GP"]
            mark["dv_y"] = B.cnt["DV"]
            mark["ac_scan"] = B.cnt["AC"]
            mark["dv_scan_u_free"] = B.cnt["DV"]
            mark[("mrep_used", c % 2)] = B.cnt["GP"]
            mark["dv_ffn"] = B.cnt["DV"]
            mark["gpc_y"] = B.cnt["GPC"]

        def yz(c):
            for m in range(16):
                B.op("dv", lambda e, m=m: e.tensor_tensor(
                    out=yz_slice(m), in0=y16[:, m * 512:(m + 1) * 512],
                    in1=siluz_bf[:, m * 512:(m + 1) * 512], op=ALU.mult))
            mark["dv_yz"] = B.cnt["DV"]

        def out_proj(c):
            banks = (b0, b1)
            for m in range(8):
                si = wload(sc_wo[m].rearrange("p kt f -> p (kt f)"), 2048)
                bk = banks[m % 2]
                wuse(si)
                bank_wait("pe", bk)
                B.wait("pe", "DV", mark["dv_yz"])
                for kt in range(16):
                    B.op("pe", lambda e, kt=kt, si=si, bk=bk: e.matmul(
                        out=bk[:, :], lhsT=wslots[si][:, kt * 128:(kt + 1) * 128],
                        rhs=yz_slice(kt), start=(kt == 0), stop=(kt == 15)))
                wdone(si)
                pe_c = B.cnt["PE"]
                B.wait("ac", "PE", pe_c)
                B.op("ac", lambda e, m=m, bk=bk: e.activation(
                    out=y16[:, m * 512:(m + 1) * 512], in_=bk[:, :], func=AF.Copy))
                bank_consumed(bk, "AC", B.cnt["AC"])
            mark["ac_opT"] = B.cnt["AC"]
            mark["pe_op"] = B.cnt["PE"]

        def merge(c):
            xslot = arnI[:, 0:1024]
            xmm = arnI[:, 1024:2048]
            xn_t = arnI[:, 2048:3072]
            mgG = arnI[:, 3072:4096]
            sq = arnJ[:, 0:1024]
            last_st = mark.get("out_store", 0)
            for tt in range(NTT):
                B.wait("gp", "PE", mark["pe_op"])
                B.wait("gp", "DV", mark.get("merge_x_used", 0))
                B.wait("gp", "SY", last_st)
                ld = B.dma("gp", xslot,
                           xin[c * L + tt * 128: c * L + (tt + 1) * 128, :])
                B.wait("ac", "GP", ld)
                B.wait("ac", "SY", last_st)
                B.wait("ac", "PE", mark["pe_op"])
                B.wait("ac", "DV", mark.get("merge_x_used", 0))
                ac_xn = tok_ln(xn_t, xslot, xmm, sq, 0, 2, 4)
                for half in range(2):
                    bank_wait("pe", bx)
                    B.wait("pe", "AC", max(mark["ac_opT"], ac_xn))
                    for g in range(4):
                        dt = half * 4 + g
                        B.op("pe", lambda e, dt=dt, g=g, tt=tt: e.transpose(
                            out=bx[:, g * 128:(g + 1) * 128],
                            in_=y16[:, dt * 512 + tt * 128: dt * 512 + (tt + 1) * 128],
                            identity=ident_f[:, :]))
                    pe_c = B.cnt["PE"]
                    B.wait("dv", "PE", pe_c)
                    B.wait("dv", "AC", ac_xn)
                    B.op("dv", lambda e, half=half: e.tensor_tensor(
                        out=xmm[:, half * 512:(half + 1) * 512], in0=bx[:, 0:512],
                        in1=xn_t[:, half * 512:(half + 1) * 512], op=ALU.add))
                    bank_consumed(bx, "DV", B.cnt["DV"])
                for half in range(2):
                    bank_wait("pe", bx)
                    B.wait("pe", "DV", mark["dv_ffn"])
                    for g in range(4):
                        m2 = half * 4 + g
                        B.op("pe", lambda e, m2=m2, g=g, tt=tt: e.transpose(
                            out=bx[:, g * 128:(g + 1) * 128],
                            in_=ffn_acc[:, m2 * 512 + tt * 128: m2 * 512 + (tt + 1) * 128],
                            identity=ident_f[:, :]))
                    pe_c = B.cnt["PE"]
                    B.wait("ac", "PE", pe_c)
                    B.wait("ac", "DV", mark.get("merge_g_used", 0))
                    B.op("ac", lambda e, half=half: e.activation(
                        out=mgG[:, half * 512:(half + 1) * 512], in_=bx[:, 0:512],
                        func=AF.Silu))
                    bank_consumed(bx, "AC", B.cnt["AC"])
                ac_g = B.cnt["AC"]
                B.wait("dv", "AC", ac_g)
                B.op("dv", lambda e: e.tensor_tensor(out=xmm, in0=xmm, in1=mgG,
                                                     op=ALU.mult))
                B.op("dv", lambda e: e.tensor_tensor(out=mgG, in0=xmm, in1=xslot,
                                                     op=ALU.add))
                dv_out = B.cnt["DV"]
                mark["merge_x_used"] = dv_out
                mark["merge_g_used"] = dv_out
                B.wait("sy", "DV", dv_out)
                last_st = B.dma("sy", out[c * L + tt * 128: c * L + (tt + 1) * 128, :],
                                mgG)
            mark["out_store"] = last_st
            mark["pe_merge"] = B.cnt["PE"]

        def do_tap(names_bufs):
            stores = 0
            for name, buf, shape, dt in names_bufs:
                tout = nc.dram_tensor("tap_" + name, shape, dt,
                                      kind="ExternalOutput").ap()
                B.wait("sy", "DV", B.cnt["DV"])
                B.wait("sy", "AC", B.cnt["AC"])
                B.wait("sy", "PE", B.cnt["PE"])
                B.wait("sy", "GP", B.cnt["GP"])
                stores = B.dma("sy", tout, buf)
            B.wait("sy", "SY", stores)

        done = False
        for c in range(NCH):
            ln_zone(c)
            if tap == "ln" and c == tap_chunk:
                do_tap([("xnT", xnT_bf[:, :], [128, 8 * 512], BF16),
                        ("ln2T", arnE[:, :], [128, 8 * 512], BF16)])
                done = True
                break
            mask_zone(c)
            if tap == "mask" and c == tap_chunk:
                do_tap([("mtok", mtok[:, :], [128, 32], F32),
                        ("mrowsT", mrowsT[:, :], [8, 512], F32)])
                done = True
                break
            in_proj(c)
            if tap == "inproj" and c == tap_chunk:
                do_tap([("xinT", arnI_bf[:, 0:8320], [128, 8320], BF16),
                        ("siluz", siluz_bf[:, :], [128, 16 * 512], BF16)])
                done = True
                break
            conv(c)
            if tap == "conv" and c == tap_chunk:
                do_tap([("u16", u16[:, :], [128, 16 * 512], F32)])
                done = True
                break
            x_proj(c)
            if tap == "xproj" and c == tap_chunk:
                do_tap([("xssmT", xssmT[:, :], [33, 512], F32)])
                done = True
                break
            scan_ffn(c)
            if tap == "scan" and c == tap_chunk:
                do_tap([("y16", y16[:, :], [128, 16 * 512], F32)])
                done = True
                break
            if tap == "hcar" and c == tap_chunk:
                do_tap([("hcar", hcar[:, :], [128, 256], F32)])
                done = True
                break
            if tap == "ffn" and c == tap_chunk:
                do_tap([("ffn_acc", ffn_acc[:, :], [128, 8 * 512], F32)])
                done = True
                break
            yz(c)
            out_proj(c)
            if tap == "outproj" and c == tap_chunk:
                do_tap([("opT", y16[:, 0:8 * 512], [128, 8 * 512], F32)])
                done = True
                break
            merge(c)

        if not done:
            B.wait("sy", "SY", mark["out_store"])

        with nc.Block() as block:
            @block.tensor
            def _(eng):
                B.replay("pe", eng)

            @block.vector
            def _(eng):
                B.replay("dv", eng)

            @block.scalar
            def _(eng):
                B.replay("ac", eng)

            @block.sync
            def _(eng):
                B.replay("sy", eng)

            @block.gpsimd
            def _(eng):
                B.replay("gp", eng)

    return nc


def kernel(**inputs):
    x = np.ascontiguousarray(np.asarray(inputs["L0_attn_x"]))
    names = dict(
        norm_w="L0_attn_norm_w", norm_b="L0_attn_norm_b", in_proj_w="L0_attn_in_proj_w",
        conv_w="L0_attn_conv_w", conv_b="L0_attn_conv_b", x_proj_w="L0_attn_x_proj_w",
        dt_w="L0_attn_dt_w", dt_b="L0_attn_dt_b", out_proj_w="L0_attn_out_proj_w",
        gate_w="L0_ffn_gate_w", sh_gate="L0_ffn_sh_gate", sh_up="L0_ffn_sh_up",
        sh_down="L0_ffn_sh_down", rt_gate="L0_ffn_rt_gate", rt_up="L0_ffn_rt_up",
        rt_down="L0_ffn_rt_down")
    common = {k: np.ascontiguousarray(np.asarray(inputs[v])) for k, v in names.items()}
    nc = build()
    in_maps = [dict(common, x=np.ascontiguousarray(x[b])) for b in range(8)]
    res = run_bass_kernel_spmd(nc, in_maps, core_ids=list(range(8)))
    return np.stack([np.asarray(r["out"]) for r in res.results], axis=0)
